# version 09
# baseline (speedup 1.0000x reference)
import contextlib
import numpy as np
import concourse.bass as bass
import concourse.mybir as mybir
from concourse.bass_utils import run_bass_kernel_spmd

F32 = mybir.dt.float32
BF16 = mybir.dt.bfloat16
U32 = mybir.dt.uint32
AX = mybir.AluOpType
AF = mybir.ActivationFunctionType
XAX = mybir.AxisListType.X

D = 1024
R = 8
TCORE = 4096
CA = 256
NCHA = TCORE // CA
NE = 32
IE = 512
G = 4
EPS1 = 1e-5
EPS2 = 1e-6

ENGS = ("sync", "gpsimd", "tensor", "vector", "scalar")

CV_LN1W, CV_LN1B, CV_LNCW, CV_LNCB, CV_LNGW, CV_LNGB, CV_BG = 0, 8, 16, 24, 32, 40, 48
CV_RBIAS = 56
CV_EGRP = 57
CV_EIOTA = 89
CV_NCOLS = 121


class P:
    def __init__(self):
        self.nc = bass.Bass("TRN2", target_bir_lowering=False)
        nc = self.nc
        dram = lambda n, s, dt=F32, k="ExternalInput": nc.dram_tensor(n, s, dt, kind=k).ap()
        self.xT = dram("xT", [D, TCORE])
        self.wt = dram("wt", [D, D])
        self.wo = dram("wo", [D, D])
        self.wg = dram("wg", [2 * D, D])
        self.rwT = dram("rwT", [D, NE])
        self.gwT = dram("gwT", [NE + 2, D, IE])
        self.uwT = dram("uwT", [NE + 2, D, IE])
        self.dwT = dram("dwT", [NE + 2, IE, D])
        self.cvec = dram("cvec", [128, CV_NCOLS])
        self.idn = dram("idn", [128, 128])
        self.ohot = dram("ohot", [NE, NE * 128], BF16)
        self.out = dram("out", [D, TCORE], k="ExternalOutput")
        self.x1h = dram("x1h", [D, TCORE], k="Internal")

    def beg(self, E):
        self.E = E
        self.sem_n = dict.fromkeys(self.sems, 0)
        self.wtgt = {}

    def new_sems(self, ctx, names):
        self.sems = {}
        for n in names:
            self.sems[n] = ctx.enter_context(self.nc.semaphore(n))

    def inc(self, sem, k, instr=None):
        self.sem_n[sem] += k
        if instr is not None:
            instr.then_inc(self.sems[sem], k)
        return self.sem_n[sem]

    def wait(self, eng, owner, sem, val=None):
        if self.E == owner:
            eng.wait_ge(self.sems[sem], self.sem_n[sem] if val is None else val)

    def on(self, owner):
        return self.E == owner

    def n(self, sem):
        return self.sem_n[sem]


def emit_setup(p, eng, t):
    if p.on("sync"):
        i1 = eng.dma_start(t["cvec"][:, :], p.cvec)
    else:
        i1 = None
    p.inc("s_ld", 16, i1)
    if p.on("gpsimd"):
        for r in range(R):
            ji = eng.iota(t["hcol8"][:, r:r + 1], [[0, 1]], base=128 * r,
                          channel_multiplier=1, allow_small_or_imprecise_dtypes=True)
    else:
        ji = None
    p.inc("s_gp", 1, ji)

    i = None
    if p.on("vector"):
        eng.wait_ge(p.sems["s_gp"], 1)
        eng.memset(t["onesr"][:, :], 1.0)
        eng.memset(t["onesc"][:, :], 1.0)
        eng.memset(t["csc"][:, 0:1], 1023.0)
        eng.memset(t["csc"][:, 1:2], 0.5)
        eng.drain()
        eng.reciprocal(t["csc"][:, 2:3], t["csc"][:, 0:1])
        eng.drain()
        eng.tensor_scalar(t["l8"][:, :], t["hcol8"][:, :], 6.0, None, AX.mult)
        eng.drain()
        i = eng.tensor_scalar(t["l8"][:, :], t["l8"][:, :], t["csc"][:, 2:3], 2.0,
                              AX.mult, AX.add)
    p.inc("s_v", 1, i)
    i = None
    if p.on("scalar"):
        eng.wait_ge(p.sems["s_v"], p.n("s_v"))
        i = eng.activation(t["csc"][:, 3:4], t["csc"][:, 1:2], AF.Ln)
    p.inc("s_sc", 1, i)
    i = None
    if p.on("vector"):
        eng.wait_ge(p.sems["s_sc"], p.n("s_sc"))
        eng.reciprocal(t["rl8"][:, :], t["l8"][:, :])
        eng.drain()
        i = eng.tensor_scalar(t["lna8"][:, :], t["rl8"][:, :], t["csc"][:, 3:4],
                              None, AX.mult)
    p.inc("s_v", 1, i)
    i = None
    if p.on("scalar"):
        eng.wait_ge(p.sems["s_v"], p.n("s_v"))
        i = eng.activation(t["a8"][:, :], t["lna8"][:, :], AF.Exp)
    p.inc("s_sc", 1, i)
    i = None
    if p.on("vector"):
        eng.wait_ge(p.sems["s_sc"], p.n("s_sc"))
        i = eng.tensor_scalar(t["b8"][:, :], t["a8"][:, :], -1.0, 1.0, AX.mult, AX.add)
        eng.drain()
    p.inc("s_v", 1, i)


def ln_stats(p, eng, t, src, b0, b1, sqn, st, mode, src_v, src_sc, bank_free):
    sem = p.sems
    ps = t["ps"]
    b_pe = p.n("s_pe")
    b_sc = p.n("s_sc")
    for r in range(R):
        sp = r % 2
        sq = t[sqn[sp]]
        i = None
        if p.on("scalar"):
            if r == 0:
                if src_v:
                    eng.wait_ge(sem["s_v"], src_v)
                if src_sc:
                    eng.wait_ge(sem["s_sc"], src_sc)
            if r >= 2:
                eng.wait_ge(sem["s_pe"], b_pe + (r - 2) + 1)
            i = eng.activation(sq[:, :], src[:, r * CA:(r + 1) * CA], AF.Square)
        p.inc("s_sc", 1, i)
        i = None
        if p.on("tensor"):
            if r == 0:
                if bank_free:
                    eng.wait_ge(sem[bank_free[0]], bank_free[1])
                if src_v:
                    eng.wait_ge(sem["s_v"], src_v)
                if src_sc:
                    eng.wait_ge(sem["s_sc"], src_sc)
            eng.matmul(ps[b0][0:1, 0:CA], t["onesc"][:, 0:1],
                       src[:, r * CA:(r + 1) * CA],
                       start=(r == 0), stop=(r == R - 1))
            eng.wait_ge(sem["s_sc"], b_sc + r + 1)
            i = eng.matmul(ps[b1][0:1, 0:CA], t["onesc"][:, 0:1], sq[:, :],
                           start=(r == 0), stop=(r == R - 1))
        p.inc("s_pe", 1, i)
    i = None
    if p.on("vector"):
        eng.wait_ge(sem["s_pe"], p.n("s_pe"))
        eng.tensor_scalar(st[0:1, 0:CA], ps[b0][0:1, 0:CA], 1.0 / D, None, AX.mult)
        eng.tensor_scalar(st[0:1, CA:2 * CA], ps[b1][0:1, 0:CA], 1.0 / D,
                          None, AX.mult)
        eng.drain()
        eng.tensor_tensor(st[0:1, 2 * CA:3 * CA], st[0:1, 0:CA], st[0:1, 0:CA],
                          AX.mult)
        eng.drain()
        eng.tensor_tensor(st[0:1, CA:2 * CA], st[0:1, CA:2 * CA],
                          st[0:1, 2 * CA:3 * CA], AX.subtract)
        eng.drain()
        if mode.startswith("fused1"):
            i = eng.tensor_scalar(st[0:1, CA:2 * CA], st[0:1, CA:2 * CA],
                                  1.0 + EPS2, EPS1 * EPS2, AX.mult, AX.add)
        else:
            i = eng.tensor_scalar(st[0:1, CA:2 * CA], st[0:1, CA:2 * CA],
                                  1.0, EPS2, AX.mult, AX.add)
    p.inc("s_v", 1, i)
    rows_v = p.n("s_v")
    i = None
    if p.on("scalar"):
        eng.wait_ge(sem["s_v"], p.n("s_v"))
        eng.activation(st[0:1, 2 * CA:3 * CA], st[0:1, CA:2 * CA], AF.Ln)
        eng.drain()
        i = eng.activation(st[0:1, CA:2 * CA], st[0:1, 2 * CA:3 * CA],
                           AF.Exp, scale=-0.5)
    p.inc("s_sc", 1, i)
    return (rows_v, p.n("s_sc"))


def ln_apply(p, eng, t, src, b0, b1, sqn, st, rows_v, wcol, bcol,
             sigmoid, out_a, out_b, res_war=None):
    sem = p.sems
    ps = t["ps"]
    i = None
    if p.on("tensor"):
        eng.wait_ge(sem["s_v"], rows_v[0])
        eng.matmul(ps[b1][:, 0:CA], t["onesr"][0:1, :], st[0:1, 0:CA],
                   start=True, stop=True)
        eng.wait_ge(sem["s_sc"], rows_v[1])
        i = eng.matmul(ps[b0][:, 0:CA], t["onesr"][0:1, :], st[0:1, CA:2 * CA],
                       start=True, stop=True)
    p.inc("s_pe", 1, i)
    bc_pe = p.n("s_pe")
    b_sc2 = p.n("s_sc")
    spr = 2 if out_b is not None else 1
    for r in range(R):
        c0 = r * CA
        sp = r % 2
        sq = t[sqn[sp]]
        i = None
        if p.on("vector"):
            eng.wait_ge(sem["s_pe"], bc_pe)
            if r == 0 and res_war is not None:
                eng.wait_ge(sem[res_war[0]], res_war[1])
            if r >= 2:
                eng.wait_ge(sem["s_sc"], b_sc2 + (r - 2) * spr + 1)
            eng.tensor_tensor(sq[:, :], src[:, c0:c0 + CA], ps[b1][:, 0:CA],
                              AX.subtract)
            eng.drain()
            i = eng.tensor_tensor(sq[:, :], sq[:, :], ps[b0][:, 0:CA], AX.mult)
            eng.drain()
        p.inc("s_v", 1, i)
        i = None
        if p.on("scalar"):
            eng.wait_ge(sem["s_v"], p.n("s_v"))
            fn = AF.Sigmoid if sigmoid else AF.Identity
            i = eng.activation(out_a[:, c0:c0 + CA], sq[:, :], fn,
                               bias=t["cvec"][:, bcol + r:bcol + r + 1],
                               scale=t["cvec"][:, wcol + r:wcol + r + 1])
        p.inc("s_sc", 1, i)
        if out_b is not None:
            i = None
            if p.on("scalar"):
                eng.drain()
                i = eng.copy(out_b[:, c0:c0 + CA], out_a[:, c0:c0 + CA])
            p.inc("s_sc", 1, i)
    return {"apply_v": p.n("s_v"), "out_sc": p.n("s_sc")}


def emit_phase_a(p, eng, t):
    sem = p.sems
    ps = t["ps"]

    def w_sl(wsb, kb, mi):
        return wsb[:, kb * D + mi * 128: kb * D + mi * 128 + 128]

    if p.on("gpsimd"):
        j1 = j2 = j3 = None
        for name, src in (("wt_sb", p.wt), ("wo_sb", p.wo)):
            sap = bass.AP(src.tensor, 0, [[D, 128], [128 * D, R], [1, D]])
            dap = bass.AP(t[name], 0, [[R * D, 128], [D, R], [1, D]])
            j = eng.dma_start(dap, sap)
            if name == "wt_sb":
                j1 = j
            else:
                j2 = j
        sap = bass.AP(p.wg.tensor, 0, [[D, 128], [128 * D, 2 * R], [1, D]])
        dap = bass.AP(t["wg_sb"], 0, [[2 * R * D, 128], [D, 2 * R], [1, D]])
        j3 = eng.dma_start(dap, sap)
    else:
        j1 = j2 = j3 = None
    p.inc("s_wA", 16, j1)
    p.inc("s_wA", 16, j2)
    p.inc("s_wA", 16, j3)
    if p.on("tensor"):
        eng.wait_ge(sem["s_wA"], p.n("s_wA"))
    if p.on("scalar") or p.on("vector"):
        eng.wait_ge(sem["s_ld"], 16)

    def proj(wsb, rhs_of, out_evac, nk, rdy_sc, bank_free, evac_wait=None):
        b_e = p.n("s_sc")
        for mi in range(R):
            bank = mi % 2
            i = None
            if p.on("tensor"):
                if mi == 0:
                    if rdy_sc:
                        eng.wait_ge(sem["s_sc"], rdy_sc)
                    if bank_free:
                        eng.wait_ge(sem["s_sc"], bank_free)
                if mi >= 2:
                    eng.wait_ge(sem["s_sc"], b_e + (mi - 2) + 1)
                for kb in range(nk):
                    i = eng.matmul(ps[bank][:, 0:CA], w_sl(wsb, kb, mi), rhs_of(kb),
                                   start=(kb == 0), stop=(kb == nk - 1))
            p.inc("s_pe", 1, i)
            i = None
            if p.on("scalar"):
                if mi == 0 and evac_wait is not None:
                    eng.wait_ge(sem[evac_wait[0]], evac_wait[1])
                eng.wait_ge(sem["s_pe"], p.n("s_pe"))
                i = out_evac(eng, mi, ps[bank])
            p.inc("s_sc", 1, i)
        return p.n("s_sc")

    def B(name, j):
        return t[name + str(j % 2)]

    state = {"ln1": 0, "lnc": 0, "lng": 0, "proj": 0,
             "mix_resfree": 0, "mix_done": 0}
    ln1_out = [None] * (NCHA + 2)
    scan_done = [0] * (NCHA + 2)
    u_done = [0] * (NCHA + 2)

    def x_load(j):
        par = j % 2
        if j >= 2:
            p.wait(eng, "sync", "s_x1s_%d" % par)
        i = None
        if p.on("sync"):
            sap = bass.AP(p.xT.tensor, j * CA, [[TCORE, 128], [128 * TCORE, R], [1, CA]])
            dap = bass.AP(B("xq", j), 0, [[R * CA, 128], [CA, R], [1, CA]])
            i = eng.dma_start(dap, sap)
        p.inc("s_x_%d" % par, 16, i)

    def ln1_full(j):
        par = j % 2
        if p.on("scalar") or p.on("tensor") or p.on("vector"):
            eng.wait_ge(sem["s_x_%d" % par], p.n("s_x_%d" % par))
        rows = ln_stats(p, eng, t, B("xq", j), 2, 3, ("sqA0", "sqA1"), t["stA"],
                        "fused1", None, None, ("s_v", state["ln1"]))
        rec = ln_apply(p, eng, t, B("xq", j), 2, 3, ("sqA0", "sqA1"), t["stA"],
                       rows, CV_LN1W, CV_LN1B, False, B("res", j), B("resb", j),
                       res_war=("s_vx1a", state["mix_resfree"]))
        state["ln1"] = rec["apply_v"]
        return rec

    def u_proj(j):
        def evac_u(eng, mi, bank, j=j):
            return eng.activation(B("u", j)[:, mi * CA:(mi + 1) * CA],
                                  bank[:, 0:CA], AF.Identity,
                                  scale=t["b8"][:, mi:mi + 1])
        return proj(t["wt_sb"],
                    lambda kb, j=j: B("resb", j)[:, kb * CA:(kb + 1) * CA],
                    evac_u, R, rdy_sc=ln1_out[j]["out_sc"],
                    bank_free=state["proj"],
                    evac_wait=("s_vx1", state["mix_done"]))

    def scan(j):
        i = None
        if p.on("vector"):
            eng.wait_ge(sem["s_sc"], u_done[j])
            for r in range(R):
                c0 = r * CA
                eng.drain()
                a_b = bass.AP(t["a8"], r, [[R, 128], [0, CA]])
                eng.tensor_tensor_scan(B("cx", j)[:, c0:c0 + CA],
                                       a_b, B("u", j)[:, c0:c0 + CA],
                                       (0.0 if j == 0 else t["carry"][:, r:r + 1]),
                                       AX.mult, AX.add)
                eng.drain()
                i = eng.tensor_copy(t["carry"][:, r:r + 1],
                                    B("cx", j)[:, c0 + CA - 1:c0 + CA])
            eng.drain()
        p.inc("s_v", 1, i)
        return p.n("s_v")

    x_load(0)
    ln1_out[0] = ln1_full(0)
    u_done[0] = u_proj(0)
    state["proj"] = u_done[0]
    scan_done[0] = scan(0)
    x_load(1)
    ln1_out[1] = ln1_full(1)

    for j in range(NCHA):
        par = j % 2
        xq = B("xq", j)
        rows_c = ln_stats(p, eng, t, B("cx", j), 4, 5, ("sq0", "sq1"), t["stC"],
                          "plain", scan_done[j], None, ("s_v", state["lnc"]))
        lnc_rec = ln_apply(p, eng, t, B("cx", j), 4, 5, ("sq0", "sq1"), t["stC"],
                           rows_c, CV_LNCW, CV_LNCB, False, B("lncxb", j), None)
        state["lnc"] = lnc_rec["apply_v"]

        def evac_o(eng, mi, bank, j=j):
            eng.copy(B("outp", j)[:, mi * CA:(mi + 1) * CA], bank[:, 0:CA])
            eng.drain()
            return eng.copy(B("outpb", j)[:, mi * CA:(mi + 1) * CA],
                            B("outp", j)[:, mi * CA:(mi + 1) * CA])
        wo_evac = proj(t["wo_sb"],
                       lambda kb, j=j: B("lncxb", j)[:, kb * CA:(kb + 1) * CA],
                       evac_o, R, rdy_sc=lnc_rec["out_sc"],
                       bank_free=state["proj"])

        def evac_g(eng, mi, bank, j=j):
            return eng.activation(B("u", j)[:, mi * CA:(mi + 1) * CA],
                                  bank[:, 0:CA], AF.Identity,
                                  bias=t["cvec"][:, CV_BG + mi:CV_BG + mi + 1])

        def wg_rhs(kb, j=j):
            return (B("resb", j)[:, kb * CA:(kb + 1) * CA] if kb < R
                    else B("outpb", j)[:, (kb - R) * CA:(kb - R + 1) * CA])
        wg_evac = proj(t["wg_sb"], wg_rhs, evac_g, 2 * R, rdy_sc=wo_evac,
                       bank_free=wo_evac,
                       evac_wait=("s_vx1", state["mix_done"]))
        state["proj"] = wg_evac

        rows_g = ln_stats(p, eng, t, B("u", j), 6, 7, ("mx0", "mx1"), t["stG"],
                          "plain", None, wg_evac, ("s_v", state["lng"]))
        if j + 1 < NCHA:
            u_done[j + 1] = u_proj(j + 1)
            state["proj"] = u_done[j + 1]
            scan_done[j + 1] = scan(j + 1)
        lng_rec = ln_apply(p, eng, t, B("u", j), 6, 7, ("mx0", "mx1"), t["stG"],
                           rows_g, CV_LNGW, CV_LNGB, True, B("u", j), None,
                           res_war=("s_vx1", state["mix_done"]))
        state["lng"] = lng_rec["apply_v"]

        b_sc = lng_rec["out_sc"]
        for r in range(R):
            c0 = r * CA
            sp = r % 2
            sq = t["mg0" if sp == 0 else "mg1"]
            i = i2 = None
            if p.on("gpsimd"):
                eng.wait_ge(sem["s_sc"], b_sc)
                eng.tensor_tensor(sq[:, :], B("outp", j)[:, c0:c0 + CA],
                                  B("res", j)[:, c0:c0 + CA], AX.subtract)
                i2 = eng.tensor_tensor(xq[:, c0:c0 + CA], xq[:, c0:c0 + CA],
                                       B("res", j)[:, c0:c0 + CA], AX.add)
                eng.drain()
                eng.tensor_tensor(sq[:, :], sq[:, :], B("u", j)[:, c0:c0 + CA],
                                  AX.mult)
                eng.drain()
                i = eng.tensor_tensor(xq[:, c0:c0 + CA], xq[:, c0:c0 + CA],
                                      sq[:, :], AX.add)
            p.inc("s_vx1a", 1, i2)
            p.inc("s_vx1", 1, i)
        state["mix_resfree"] = p.n("s_vx1a")
        state["mix_done"] = p.n("s_vx1")
        p.wait(eng, "sync", "s_vx1")
        i = None
        if p.on("sync"):
            sap = bass.AP(xq, 0, [[R * CA, 128], [CA, R], [1, CA]])
            dap = bass.AP(p.x1h.tensor, j * CA, [[TCORE, 128], [128 * TCORE, R], [1, CA]])
            i = eng.dma_start(dap, sap)
        p.inc("s_x1s_%d" % par, 16, i)
        if j + 2 < NCHA:
            x_load(j + 2)
            ln1_out[j + 2] = ln1_full(j + 2)
    for par in range(2):
        p.wait(eng, "sync", "s_x1s_%d" % par)


TQ = 1024
NQ = TCORE // TQ
CT = 512
NTC = TQ // CT
NEX = NE + 2
NGRP = NEX // 2
NTT = TQ // 128


def grp_experts(g):
    return (NE, NE + 1) if g == 0 else (2 * (g - 1), 2 * (g - 1) + 1)


def emit_phase_b(p, eng, t):
    sem = p.sems
    ps = t["ps"]

    def mtile(buf, r, tc):
        return buf[:, r * TQ + tc * CT: r * TQ + tc * CT + CT]

    i1 = i2 = i3 = None
    if p.on("sync"):
        i1 = eng.dma_start(t["idn"][:, :], p.idn)
        i2 = eng.dma_start(t["ohot"][:, :], p.ohot)
        sap = bass.AP(p.rwT.tensor, 0, [[NE, 128], [128 * NE, R], [1, NE]])
        dap = bass.AP(t["rw_sb"], 0, [[R * NE, 128], [NE, R], [1, NE]])
        i3 = eng.dma_start(dap, sap)
    p.inc("s_ld", 16, i1)
    p.inc("s_ld", 16, i2)
    p.inc("s_ld", 16, i3)
    if p.on("tensor") or p.on("vector"):
        eng.wait_ge(sem["s_ld"], p.n("s_ld"))

    for q in range(NQ):
        Q0 = q * TQ
        q_fwg = [p.n("s_fwg0"), p.n("s_fwg1")]
        q_fwd = p.n("s_fwd")

        def load_grp(g):
            es = grp_experts(g)
            for el in range(2):
                e = es[el]
                par = el
                gw = t["gw0"] if par == 0 else t["gw1"]
                uw = t["uw0"] if par == 0 else t["uw1"]
                slot = (2 * g + el) % 4
                iA = iB = iC = None
                if p.on("gpsimd"):
                    if g >= 1 or q > 0:
                        eng.wait_ge(sem["s_fwg%d" % par], q_fwg[par] + g)
                    if g >= 2:
                        eng.wait_ge(sem["s_fwd"], q_fwd + g - 1)
                    elif q > 0:
                        eng.wait_ge(sem["s_fwd"], q_fwd)
                    sap = bass.AP(p.gwT.tensor, e * D * IE,
                                  [[IE, 128], [128 * IE, R], [1, IE]])
                    dap = bass.AP(gw, 0, [[R * IE, 128], [IE, R], [1, IE]])
                    iA = eng.dma_start(dap, sap)
                    sap = bass.AP(p.uwT.tensor, e * D * IE,
                                  [[IE, 128], [128 * IE, R], [1, IE]])
                    dap = bass.AP(uw, 0, [[R * IE, 128], [IE, R], [1, IE]])
                    iB = eng.dma_start(dap, sap)
                    sap = bass.AP(p.dwT.tensor, e * IE * D,
                                  [[D, 128], [128 * D, 4], [1, D]])
                    dap = bass.AP(t["dw"], slot * 4 * D,
                                  [[16 * D, 128], [D, 4], [1, D]])
                    iC = eng.dma_start(dap, sap)
                p.inc("s_wg%d" % par, 16, iA)
                p.inc("s_wu%d" % par, 16, iB)
                p.inc("s_wd%d" % slot, 16, iC)
                p.wtgt[(q, g, "g", par)] = p.n("s_wg%d" % par)
                p.wtgt[(q, g, "u", par)] = p.n("s_wu%d" % par)
                p.wtgt[(q, g, "d", slot)] = p.n("s_wd%d" % slot)

        for tcl in range(NTC):
            T0 = Q0 + tcl * CT
            p.wait(eng, "sync", "s_v")
            i = None
            if p.on("sync"):
                sap = bass.AP(p.x1h.tensor, T0, [[TCORE, 128], [128 * TCORE, R], [1, CT]])
                dap = bass.AP(t["x1t"], 0, [[R * CT, 128], [CT, R], [1, CT]])
                i = eng.dma_start(dap, sap)
            p.inc("s_x1r", 16, i)
            if p.on("scalar") or p.on("tensor") or p.on("vector"):
                eng.wait_ge(sem["s_x1r"], p.n("s_x1r"))
            b_pe = p.n("s_pe")
            b_sc = p.n("s_sc")
            for r in range(R):
                sp = r % 2
                sq = t["sq0" if sp == 0 else "sq1"]
                i = None
                if p.on("scalar"):
                    if r == 0:
                        eng.wait_ge(sem["s_v"], p.n("s_v"))
                    if r >= 2:
                        eng.wait_ge(sem["s_pe"], b_pe + (r - 2) + 1)
                    i = eng.activation(sq[:, :], t["x1t"][:, r * CT:(r + 1) * CT],
                                       AF.Square)
                p.inc("s_sc", 1, i)
                if r == 0:
                    p.wait(eng, "tensor", "s_v")
                i = None
                if p.on("tensor"):
                    eng.matmul(ps[4][0:1, :], t["onesc"][:, 0:1],
                               t["x1t"][:, r * CT:(r + 1) * CT],
                               start=(r == 0), stop=(r == R - 1))
                    eng.wait_ge(sem["s_sc"], b_sc + r + 1)
                    i = eng.matmul(ps[5][0:1, :], t["onesc"][:, 0:1], sq[:, :],
                                   start=(r == 0), stop=(r == R - 1))
                p.inc("s_pe", 1, i)
            i = None
            if p.on("vector"):
                eng.wait_ge(sem["s_pe"], p.n("s_pe"))
                st = t["strowB"]
                eng.tensor_scalar(st[0:1, 0:CT], ps[4][0:1, :], 1.0 / D, None, AX.mult)
                eng.tensor_scalar(st[0:1, CT:2 * CT], ps[5][0:1, :], 1.0 / D,
                                  None, AX.mult)
                eng.drain()
                eng.tensor_tensor(st[0:1, 2 * CT:3 * CT], st[0:1, 0:CT],
                                  st[0:1, 0:CT], AX.mult)
                eng.drain()
                eng.tensor_tensor(st[0:1, CT:2 * CT], st[0:1, CT:2 * CT],
                                  st[0:1, 2 * CT:3 * CT], AX.subtract)
                eng.drain()
                i = eng.tensor_scalar(st[0:1, CT:2 * CT], st[0:1, CT:2 * CT],
                                      1.0, EPS1, AX.mult, AX.add)
            p.inc("s_v", 1, i)
            i = None
            if p.on("scalar"):
                eng.wait_ge(sem["s_v"], p.n("s_v"))
                eng.activation(t["strowB"][0:1, 2 * CT:3 * CT],
                               t["strowB"][0:1, CT:2 * CT], AF.Ln)
                eng.drain()
                i = eng.activation(t["strowB"][0:1, CT:2 * CT],
                                   t["strowB"][0:1, 2 * CT:3 * CT], AF.Exp, scale=-0.5)
            p.inc("s_sc", 1, i)
            i = None
            if p.on("vector"):
                eng.wait_ge(sem["s_sc"], p.n("s_sc"))
                i = eng.tensor_tensor(t["strowB"][0:1, 0:CT], t["strowB"][0:1, 0:CT],
                                      t["strowB"][0:1, CT:2 * CT], AX.mult)
            p.inc("s_v", 1, i)
            i = None
            if p.on("tensor"):
                eng.wait_ge(sem["s_v"], p.n("s_v"))
                eng.wait_ge(sem["s_v2"], p.n("s_v2"))
                eng.matmul(ps[6][:, :], t["onesr"][0:1, :],
                           t["strowB"][0:1, CT:2 * CT], start=True, stop=True)
                i = eng.matmul(ps[7][:, :], t["onesr"][0:1, :],
                               t["strowB"][0:1, 0:CT], start=True, stop=True)
            p.inc("s_pe", 1, i)
            b_pe2 = p.n("s_pe")
            for r in range(R):
                sp = r % 2
                sq = t["sq0" if sp == 0 else "sq1"]
                i = None
                if p.on("vector"):
                    eng.wait_ge(sem["s_pe"], b_pe2 if r < 2 else b_pe2 + (r - 2) + 1)
                    eng.tensor_tensor(sq[:, :], t["x1t"][:, r * CT:(r + 1) * CT],
                                      ps[6][:, :], AX.mult)
                    eng.drain()
                    eng.tensor_tensor(sq[:, :], sq[:, :], ps[7][:, :], AX.subtract)
                    eng.drain()
                    i = eng.tensor_copy(mtile(t["x2b"], r, tcl), sq[:, :])
                    eng.drain()
                p.inc("s_v", 1, i)
                if r == R - 1:
                    d = None
                    if p.on("vector"):
                        d = eng.drain()
                    p.inc("s_v2", 1, d)
                i = None
                if p.on("tensor"):
                    eng.wait_ge(sem["s_v"], p.n("s_v"))
                    i = eng.matmul(ps[3][0:NE, :], t["rw_sb"][:, r * NE:(r + 1) * NE],
                                   sq[:, :], start=(r == 0), stop=(r == R - 1))
                p.inc("s_pe", 1, i)
            i = None
            if p.on("scalar"):
                eng.wait_ge(sem["s_pe"], p.n("s_pe"))
                i = eng.activation(t["scores"][0:NE, tcl * CT:(tcl + 1) * CT],
                                   ps[3][0:NE, :], AF.Sigmoid)
            p.inc("s_sc", 1, i)
            i = None
            if p.on("vector"):
                eng.wait_ge(sem["s_sc"], p.n("s_sc"))
                i = eng.tensor_scalar(t["sfc"][0:NE, tcl * CT:(tcl + 1) * CT],
                                      t["scores"][0:NE, tcl * CT:(tcl + 1) * CT],
                                      t["cvec"][0:NE, CV_RBIAS:CV_RBIAS + 1],
                                      None, AX.add)
            p.inc("s_v", 1, i)
        v_b1_done = p.n("s_v")
        sc_b1_done = p.n("s_sc")

        load_grp(0)

        for tt in range(NTT):
            c0 = tt * 128
            i = None
            if p.on("tensor"):
                eng.wait_ge(sem["s_v"], v_b1_done)
                eng.wait_ge(sem["s_sc"], sc_b1_done)
                if tt >= 1:
                    eng.wait_ge(sem["s_v3"], p.n("s_v3"))
                eng.transpose(ps[4][:, 0:NE], t["sfc"][0:NE, c0:c0 + 128],
                              t["idn"][0:NE, 0:NE])
                i = eng.transpose(ps[5][:, 0:NE], t["scores"][0:NE, c0:c0 + 128],
                                  t["idn"][0:NE, 0:NE])
            p.inc("s_pe", 1, i)
            i = None
            if p.on("vector"):
                eng.wait_ge(sem["s_pe"], p.n("s_pe"))
                eng.tensor_copy(t["sfc_tm"][:, tt * 32:(tt + 1) * 32], ps[4][:, 0:NE])
                eng.drain()
                i = eng.tensor_copy(t["sc_tm"][:, tt * 32:(tt + 1) * 32],
                                    ps[5][:, 0:NE])
                eng.drain()
            p.inc("s_v3", 1, i)
        trv = []
        for tt in range(NTT):
            sfct = t["sfc_tm"][:, tt * 32:(tt + 1) * 32]
            sct = t["sc_tm"][:, tt * 32:(tt + 1) * 32]
            w32t = t["w32tm"][:, tt * 32:(tt + 1) * 32]
            i = None
            if p.on("vector"):
                for gi in range(4):
                    eng.max(t["m8"][:, :], sfct[:, gi * 8:(gi + 1) * 8])
                    eng.drain()
                    eng.tensor_tensor(t["gs8"][:, gi:gi + 1], t["m8"][:, 0:1],
                                      t["m8"][:, 1:2], AX.add)
                    eng.drain()
                eng.max(t["m8"][:, :], t["gs8"][:, :])
                eng.drain()
                eng.max_index(t["gidxu"][:, :], t["m8"][:, :], t["gs8"][:, :])
                eng.drain()
                eng.tensor_copy(t["gidxf"][:, :], t["gidxu"][:, :])
                eng.drain()
                eng.tensor_scalar(t["eqa"][:, :], t["cvec"][:, CV_EGRP:CV_EGRP + 32],
                                  t["gidxf"][:, 0:1], None, AX.is_equal)
                eng.tensor_scalar(t["eqb"][:, :], t["cvec"][:, CV_EGRP:CV_EGRP + 32],
                                  t["gidxf"][:, 1:2], None, AX.is_equal)
                eng.drain()
                eng.tensor_tensor(t["eqa"][:, :], t["eqa"][:, :], t["eqb"][:, :], AX.add)
                eng.drain()
                eng.tensor_tensor(t["tmp32"][:, :], sfct, t["eqa"][:, :], AX.mult)
                eng.drain()
                eng.max(t["m8"][:, :], t["tmp32"][:, :])
                eng.drain()
                eng.max_index(t["idxu"][:, :], t["m8"][:, :], t["tmp32"][:, :])
                eng.drain()
                eng.tensor_copy(t["idxf"][:, :], t["idxu"][:, :])
                eng.drain()
                for k in range(4):
                    eng.tensor_scalar(t["eqm"][:, :],
                                      t["cvec"][:, CV_EIOTA:CV_EIOTA + 32],
                                      t["idxf"][:, k:k + 1], None, AX.is_equal)
                    eng.drain()
                    eng.tensor_tensor(t["eqm"][:, :], t["eqm"][:, :], sct, AX.mult)
                    eng.drain()
                    eng.reduce_sum(t["skc"][:, k:k + 1], t["eqm"][:, :],
                                   axis=mybir.AxisListType.X)
                    eng.drain()
                eng.tensor_tensor(t["skc"][:, 4:5], t["skc"][:, 0:1], t["skc"][:, 1:2],
                                  AX.add)
                eng.tensor_tensor(t["skc"][:, 5:6], t["skc"][:, 2:3], t["skc"][:, 3:4],
                                  AX.add)
                eng.drain()
                eng.tensor_tensor(t["skc"][:, 4:5], t["skc"][:, 4:5], t["skc"][:, 5:6],
                                  AX.add)
                eng.drain()
                eng.tensor_scalar(t["skc"][:, 4:5], t["skc"][:, 4:5], 1e-20, None,
                                  AX.add)
                eng.drain()
                eng.reciprocal(t["skc"][:, 5:6], t["skc"][:, 4:5])
                eng.drain()
                eng.tensor_scalar(t["skc"][:, 5:6], t["skc"][:, 5:6], 2.5, None,
                                  AX.mult)
                eng.drain()
                for k in range(4):
                    eng.tensor_tensor(t["skc"][:, 8 + k:9 + k], t["skc"][:, k:k + 1],
                                      t["skc"][:, 5:6], AX.mult)
                eng.drain()
                eng.memset(w32t, 0.0)
                eng.drain()
                for k in range(4):
                    eng.tensor_scalar(t["eqm"][:, :],
                                      t["cvec"][:, CV_EIOTA:CV_EIOTA + 32],
                                      t["idxf"][:, k:k + 1], None, AX.is_equal)
                    eng.drain()
                    eng.tensor_scalar(t["eqm"][:, :], t["eqm"][:, :],
                                      t["skc"][:, 8 + k:9 + k], None, AX.mult)
                    eng.drain()
                    eng.tensor_tensor(w32t, w32t, t["eqm"][:, :], AX.add)
                    eng.drain()
                i = eng.drain()
            p.inc("s_v", 1, i)
            trv.append(p.n("s_v"))

        p.wait(eng, "sync", "s_outs")
        i = None
        if p.on("sync"):
            sap = bass.AP(p.x1h.tensor, Q0, [[TCORE, 128], [128 * TCORE, R], [1, TQ]])
            dap = bass.AP(t["mixed"], 0, [[R * TQ, 128], [TQ, R], [1, TQ]])
            i = eng.dma_start(dap, sap)
        p.inc("s_mxl", 16, i)
        mxl_rec = p.n("s_mxl")

        set0_free = v_b1_done
        set1_free = None
        s_v3_all = p.n("s_v3")
        for g in range(NGRP):
            pseudo = (g == 0)
            if g + 1 <= NGRP - 1:
                load_grp(g + 1)
            its = [(0, 0), (1, 0), (0, 1), (1, 1)]
            h2_v = [None] * 4
            for it, (el, tc) in enumerate(its):
                bs = (it % 2) * 4
                par = el
                gw = t["gw0"] if par == 0 else t["gw1"]
                uw = t["uw0"] if par == 0 else t["uw1"]
                sg = t["sg0"] if par == 0 else t["sg1"]
                b_gate = p.n("s_pe")
                for mi in range(4):
                    i = None
                    if p.on("tensor"):
                        if mi == 0:
                            if it == 0:
                                eng.wait_ge(sem["s_v"], set0_free)
                            elif it == 1:
                                if pseudo:
                                    eng.wait_ge(sem["s_v"], v_b1_done)
                                    eng.wait_ge(sem["s_v3"], s_v3_all)
                                else:
                                    eng.wait_ge(sem["s_v"], set1_free)
                            else:
                                eng.wait_ge(sem["s_v"], h2_v[it - 2][0])
                            eng.wait_ge(sem["s_wg%d" % par],
                                        p.wtgt[(q, g, "g", par)])
                            if tc == 0:
                                eng.wait_ge(sem["s_wu%d" % par],
                                            p.wtgt[(q, g, "u", par)])
                            if g == 0 and it == 0:
                                eng.wait_ge(sem["s_sc"], sc_b1_done)
                        for kb in range(R):
                            i = eng.matmul(
                                ps[bs + mi][:, :],
                                gw[:, kb * IE + mi * 128: kb * IE + mi * 128 + 128],
                                mtile(t["x2b"], kb, tc),
                                start=(kb == 0), stop=(kb == R - 1))
                    p.inc("s_pe", 1, i)
                b_sc2 = p.n("s_sc2")
                for mi in range(4):
                    i = None
                    if p.on("scalar"):
                        eng.wait_ge(sem["s_pe"], b_gate + mi + 1)
                        i = eng.activation(sg[:, mi * CT:(mi + 1) * CT],
                                           ps[bs + mi][:, :], AF.Silu)
                    p.inc("s_sc2", 1, i)
                b_up = p.n("s_pe")
                for mi in range(4):
                    i = None
                    if p.on("tensor"):
                        eng.wait_ge(sem["s_sc2"], b_sc2 + mi + 1)
                        for kb in range(R):
                            i = eng.matmul(
                                ps[bs + mi][:, :],
                                uw[:, kb * IE + mi * 128: kb * IE + mi * 128 + 128],
                                mtile(t["x2b"], kb, tc),
                                start=(kb == 0), stop=(kb == R - 1))
                    p.inc("s_pe", 1, i)
                for mi in range(4):
                    hsl = t["hidden"][:, (tc * 8 + el * 4 + mi) * CT:
                                      (tc * 8 + el * 4 + mi + 1) * CT]
                    i = None
                    if p.on("vector"):
                        eng.wait_ge(sem["s_pe"], b_up + mi + 1)
                        eng.wait_ge(sem["s_sc2"], p.n("s_sc2"))
                        eng.tensor_tensor(hsl, sg[:, mi * CT:(mi + 1) * CT],
                                          ps[bs + mi][:, :], AX.mult)
                        eng.drain()
                        if not pseudo:
                            idx = el * 2 + tc
                            eng.tensor_tensor(
                                hsl, hsl,
                                t["wbc_sb"][:, idx * CT:(idx + 1) * CT], AX.mult)
                            eng.drain()
                        i = eng.drain()
                    p.inc("s_v", 1, i)
                h2_v[it] = (p.n("s_v"), p.n("s_hg"))
                if tc == 1:
                    d = None
                    if p.on("vector"):
                        d = eng.drain()
                    p.inc("s_fwg%d" % par, 1, d)
            evac_v = {}
            for tc, half in ((0, 1), (0, 0), (1, 1), (1, 0)):
                if (tc, half) == (1, 0) and g + 1 <= NGRP - 1:
                    if g == 0:
                        for tt in range(NTT):
                            i = None
                            if p.on("tensor"):
                                eng.wait_ge(sem["s_v"], trv[tt])
                                if tt == 0:
                                    eng.wait_ge(sem["s_v"], evac_v[(0, 0)])
                                if tt >= 1:
                                    eng.wait_ge(sem["s_v4"], p.n("s_v4"))
                                i = eng.transpose(
                                    ps[6][0:NE, 0:128],
                                    t["w32tm"][:, tt * 32:(tt + 1) * 32],
                                    t["idn"][:, :])
                            p.inc("s_pe", 1, i)
                            i = None
                            if p.on("vector"):
                                eng.wait_ge(sem["s_pe"], p.n("s_pe"))
                                i = eng.tensor_copy(
                                    t["w32cm"][0:NE, tt * 128:(tt + 1) * 128],
                                    ps[6][0:NE, 0:128])
                                eng.drain()
                            p.inc("s_v4", 1, i)
                    for idx in range(4):
                        elw, tcw = idx // 2, idx % 2
                        e_r = 2 * g + elw
                        i = None
                        if p.on("tensor"):
                            if idx == 0:
                                eng.wait_ge(sem["s_v4"], p.n("s_v4"))
                                eng.wait_ge(sem["s_v"], evac_v[(0, 0)])
                            i = eng.matmul(
                                ps[4 + idx][:, :],
                                t["ohot"][0:NE, e_r * 128:(e_r + 1) * 128],
                                t["w32cm"][0:NE, tcw * CT:(tcw + 1) * CT],
                                start=True, stop=True)
                        p.inc("s_pe", 1, i)
                        i = None
                        if p.on("vector"):
                            eng.wait_ge(sem["s_pe"], p.n("s_pe"))
                            i = eng.tensor_copy(
                                t["wbc_sb"][:, idx * CT:(idx + 1) * CT],
                                ps[4 + idx][:, :])
                            eng.drain()
                        p.inc("s_v", 1, i)
                    wbc_next = p.n("s_v")
                bs = 0 if half == 1 else 4
                mos = range(half * 4, half * 4 + 4)
                for mo_i, mo in enumerate(mos):
                    ii = None
                    if p.on("tensor"):
                        if mo_i == 0:
                            eng.wait_ge(sem["s_wd%d" % ((2 * g) % 4)],
                                        p.wtgt[(q, g, "d", (2 * g) % 4)])
                            eng.wait_ge(sem["s_wd%d" % ((2 * g + 1) % 4)],
                                        p.wtgt[(q, g, "d", (2 * g + 1) % 4)])
                            if tc == 0 and half == 1:
                                eng.wait_ge(sem["s_v"], h2_v[2][0])
                                eng.wait_ge(sem["s_hg"], h2_v[2][1])
                            elif tc == 0 and half == 0:
                                eng.wait_ge(sem["s_v"], h2_v[3][0])
                                eng.wait_ge(sem["s_hg"], h2_v[3][1])
                            elif tc == 1 and half == 1:
                                eng.wait_ge(sem["s_v"], evac_v[(0, 1)])
                                eng.wait_ge(sem["s_v"], h2_v[3][0])
                                eng.wait_ge(sem["s_hg"], h2_v[3][1])
                            else:
                                eng.wait_ge(sem["s_v"], wbc_next
                                            if g + 1 <= NGRP - 1
                                            else evac_v[(0, 0)])
                                eng.wait_ge(sem["s_hg"], h2_v[3][1])
                        for el in range(2):
                            e_slot = (2 * g + el) % 4
                            for ki in range(4):
                                ii = eng.matmul(
                                    ps[bs + mo_i][:, :],
                                    t["dw"][:, e_slot * 4 * D + ki * D + mo * 128:
                                            e_slot * 4 * D + ki * D + mo * 128 + 128],
                                    t["hidden"][:, (tc * 8 + el * 4 + ki) * CT:
                                                (tc * 8 + el * 4 + ki + 1) * CT],
                                    start=(el == 0 and ki == 0),
                                    stop=(el == 1 and ki == 3))
                    p.inc("s_pe", 1, ii)
                i = None
                if p.on("vector"):
                    if g == 0 and (tc, half) == (0, 1):
                        eng.wait_ge(sem["s_mxl"], mxl_rec)
                    eng.wait_ge(sem["s_pe"], p.n("s_pe"))
                    for mo_i, mo in enumerate(mos):
                        eng.tensor_tensor(mtile(t["mixed"], mo, tc),
                                          mtile(t["mixed"], mo, tc),
                                          ps[bs + mo_i][:, :], AX.add)
                        eng.drain()
                    i = eng.drain()
                p.inc("s_v", 1, i)
                evac_v[(tc, half)] = p.n("s_v")
            set0_free = evac_v[(1, 1)]
            set1_free = evac_v[(1, 0)]
            d = None
            if p.on("vector"):
                d = eng.drain()
            p.inc("s_fwd", 1, d)

        v_b4_done = p.n("s_v")
        p.wait(eng, "sync", "s_v", v_b4_done)
        i = None
        if p.on("sync"):
            sap = bass.AP(t["mixed"], 0, [[R * TQ, 128], [TQ, R], [1, TQ]])
            dap = bass.AP(p.out.tensor, Q0, [[TCORE, 128], [128 * TCORE, R], [1, TQ]])
            i = eng.dma_start(dap, sap)
        p.inc("s_outs", 16, i)
    p.wait(eng, "sync", "s_outs")

def build(include_ffn=False):
    p = P()
    nc = p.nc
    with contextlib.ExitStack() as ctx:
        sb = lambda n, s, dt=F32: ctx.enter_context(nc.sbuf_tensor(n, s, dt))
        t = {}
        t["cvec"] = sb("cvec_s", [128, CV_NCOLS])
        t["hcol8"] = sb("hcol8", [128, R])
        t["l8"] = sb("l8", [128, R])
        t["rl8"] = sb("rl8", [128, R])
        t["lna8"] = sb("lna8", [128, R])
        t["a8"] = sb("a8", [128, R])
        t["b8"] = sb("b8", [128, R])
        t["csc"] = sb("csc", [128, 4])
        t["onesr"] = sb("onesr", [1, 128])
        t["onesc"] = sb("onesc", [128, 1])
        t["carry"] = sb("carry", [128, R])
        sems = ["s_ld", "s_wA", "s_gp", "s_v", "s_v2", "s_v3", "s_v4", "s_sc",
                "s_sc2", "s_pe", "s_vx1", "s_x_0", "s_x_1", "s_x1s_0", "s_x1s_1",
                "s_out", "s_wg0", "s_wg1", "s_wu0", "s_wu1", "s_wd0", "s_wd1",
                "s_wd2", "s_wd3", "s_fwg0", "s_fwg1", "s_fwd",
                "s_x1r", "s_vq", "s_outs", "s_vx1a", "s_mxl", "s_hg"]
        p.new_sems(ctx, sems)

        with contextlib.ExitStack() as c0:
            blk = c0.enter_context(nc.Block())
            for E in ENGS:
                p.beg(E)
                getattr(blk, E)(lambda eng: emit_setup(p, eng, t))
            counts = dict(p.sem_n)

        with contextlib.ExitStack() as c1:
            t["wt_sb"] = c1.enter_context(nc.sbuf_tensor("wt_sb", [128, R * D], BF16))
            t["wo_sb"] = c1.enter_context(nc.sbuf_tensor("wo_sb", [128, R * D], BF16))
            t["wg_sb"] = c1.enter_context(nc.sbuf_tensor("wg_sb", [128, 2 * R * D], BF16))
            for n, s, dt in (
                ("xq0", [128, R * CA], F32), ("xq1", [128, R * CA], F32),
                ("res0", [128, R * CA], F32), ("res1", [128, R * CA], F32),
                ("resb0", [128, R * CA], BF16), ("resb1", [128, R * CA], BF16),
                ("u0", [128, R * CA], F32), ("u1", [128, R * CA], F32),
                ("cx0", [128, R * CA], F32), ("cx1", [128, R * CA], F32),
                ("lncxb0", [128, R * CA], BF16), ("lncxb1", [128, R * CA], BF16),
                ("outp0", [128, R * CA], F32), ("outp1", [128, R * CA], F32),
                ("outpb0", [128, R * CA], BF16), ("outpb1", [128, R * CA], BF16),
                ("sq0", [128, CA], F32), ("sq1", [128, CA], F32),
                ("sqA0", [128, CA], F32), ("sqA1", [128, CA], F32),
                ("mx0", [128, CA], F32), ("mx1", [128, CA], F32),
                ("mg0", [128, CA], F32), ("mg1", [128, CA], F32),
                ("stA", [1, 3 * CA], F32), ("stC", [1, 3 * CA], F32),
                ("stG", [1, 3 * CA], F32),
            ):
                t[n] = c1.enter_context(nc.sbuf_tensor(n, s, dt))
            t["ps"] = [c1.enter_context(nc.psum_tensor("psA%d" % i, [128, 512], F32))
                       for i in range(8)]
            blk = c1.enter_context(nc.Block())
            for E in ENGS:
                p.beg(E)
                p.sem_n.update(counts)
                getattr(blk, E)(lambda eng: emit_phase_a(p, eng, t))
            counts = dict(p.sem_n)

        if not include_ffn:
            with contextlib.ExitStack() as c2:
                blk = c2.enter_context(nc.Block())
                for E in ENGS:
                    p.beg(E)
                    p.sem_n.update(counts)
                    def f(eng, E=E):
                        i = None
                        if p.on("sync"):
                            i = eng.dma_start(p.out, p.x1h)
                        p.inc("s_out", 16, i)
                        p.wait(eng, "sync", "s_out")
                    getattr(blk, E)(f)
        else:
            with contextlib.ExitStack() as c2:
                sb2 = lambda n, s_, dt=F32: c2.enter_context(nc.sbuf_tensor(n, s_, dt))
                t["idn"] = sb2("idn_s", [128, 128])
                t["ohot"] = sb2("ohot_s", [NE, NE * 128], BF16)
                t["rw_sb"] = sb2("rw_sb", [128, R * NE])
                t["x2b"] = sb2("x2b", [128, R * TQ], BF16)
                t["mixed"] = sb2("mixed", [128, R * TQ])
                t["x1t"] = sb2("x1t", [128, R * CT])
                t["hidden"] = sb2("hidden", [128, 16 * CT], BF16)
                t["gw0"] = sb2("gw0", [128, R * IE], BF16)
                t["gw1"] = sb2("gw1", [128, R * IE], BF16)
                t["uw0"] = sb2("uw0", [128, R * IE], BF16)
                t["uw1"] = sb2("uw1", [128, R * IE], BF16)
                t["dw"] = sb2("dw", [128, 16 * D], BF16)
                t["sg0"] = sb2("sg0", [128, 4 * CT], BF16)
                t["sg1"] = sb2("sg1", [128, 4 * CT], BF16)
                t["wbc_sb"] = sb2("wbc_sb", [128, 4 * CT], BF16)
                t["scores"] = sb2("scores", [NE, TQ])
                t["sfc"] = sb2("sfc", [NE, TQ])
                t["w32cm"] = sb2("w32cm", [NE, TQ], BF16)
                t["w32tm"] = sb2("w32tm", [128, NTT * 32])
                t["sfc_tm"] = sb2("sfc_tm", [128, NTT * 32])
                t["sc_tm"] = sb2("sc_tm", [128, NTT * 32])
                t["m8"] = sb2("m8", [128, 8])
                t["gs8"] = sb2("gs8", [128, 8])
                t["gidxu"] = sb2("gidxu", [128, 8], U32)
                t["gidxf"] = sb2("gidxf", [128, 8])
                t["idxu"] = sb2("idxu", [128, 8], U32)
                t["idxf"] = sb2("idxf", [128, 8])
                t["eqa"] = sb2("eqa", [128, 32])
                t["eqb"] = sb2("eqb", [128, 32])
                t["eqm"] = sb2("eqm", [128, 32])
                t["tmp32"] = sb2("tmp32", [128, 32])
                t["skc"] = sb2("skc", [128, 16])
                t["strowB"] = sb2("strowB", [1, 4 * CT])
                t["sq0"] = sb2("sqB0", [128, CT])
                t["sq1"] = sb2("sqB1", [128, CT])
                t["ps"] = [c2.enter_context(nc.psum_tensor("psB%d" % i, [128, 512], F32))
                           for i in range(8)]
                blk = c2.enter_context(nc.Block())
                for E in ENGS:
                    p.beg(E)
                    p.sem_n.update(counts)
                    def f(eng, E=E):
                        if p.on("vector"):
                            eng.memset(t["gs8"][:, 4:8], -1e30)
                            eng.drain()
                        emit_phase_b(p, eng, t)
                    getattr(blk, E)(f)
    return p


def pack_cvec(ln1_w, ln1_b, lnc_w, lnc_b, lng_w, lng_b, bg, rbias):
    cv = np.zeros((128, CV_NCOLS), np.float32)
    for base, v in ((CV_LN1W, ln1_w), (CV_LN1B, ln1_b), (CV_LNCW, lnc_w),
                    (CV_LNCB, lnc_b), (CV_LNGW, lng_w), (CV_LNGB, lng_b),
                    (CV_BG, bg)):
        cv[:, base:base + 8] = np.asarray(v, np.float32).reshape(8, 128).T
    cv[0:NE, CV_RBIAS] = np.asarray(rbias, np.float32)
    cv[:, CV_EGRP:CV_EGRP + 32] = (np.arange(32)[None, :] // 8).astype(np.float32)
    cv[:, CV_EIOTA:CV_EIOTA + 32] = np.arange(32)[None, :].astype(np.float32)
    return cv


def _ext_gu(ew, sw):
    ew = np.asarray(ew, np.float32).transpose(0, 2, 1)
    swT = np.asarray(sw, np.float32).T
    ps = np.stack([swT[:, 0:IE], swT[:, IE:2 * IE]], axis=0)
    return np.ascontiguousarray(np.concatenate([ew, ps], axis=0))


def _ext_d(ew, sw):
    ew = np.asarray(ew, np.float32).transpose(0, 2, 1)
    swT = np.asarray(sw, np.float32).T
    ps = np.stack([swT[0:IE, :], swT[IE:2 * IE, :]], axis=0)
    return np.ascontiguousarray(np.concatenate([ew, ps], axis=0))


def make_inputs(inp, core):
    import ml_dtypes
    x = np.asarray(inp["L0_attn_x"], np.float32).reshape(8 * TCORE, D)
    xT = np.ascontiguousarray(x[core * TCORE:(core + 1) * TCORE, :].T)
    cv = pack_cvec(inp["L0_attn_ln1_w"], inp["L0_attn_ln1_b"],
                   inp["L0_attn_lnc_w"], inp["L0_attn_lnc_b"],
                   inp["L0_attn_lng_w"], inp["L0_attn_lng_b"],
                   inp["L0_attn_bg"], inp["L0_ffn_router_bias"])
    ohot = np.zeros((NE, NE * 128), ml_dtypes.bfloat16)
    for e in range(NE):
        ohot[e, e * 128:(e + 1) * 128] = 1
    c = np.ascontiguousarray
    m = {
        "xT": xT,
        "wt": c(np.asarray(inp["L0_attn_Wt"], np.float32)),
        "wo": c(np.asarray(inp["L0_attn_Wo"], np.float32)),
        "wg": c(np.asarray(inp["L0_attn_Wg"], np.float32)),
        "rwT": c(np.asarray(inp["L0_ffn_router_weight"], np.float32).T),
        "gwT": _ext_gu(inp["L0_ffn_gate_w"], inp["L0_ffn_shared_gate_w"]),
        "uwT": _ext_gu(inp["L0_ffn_up_w"], inp["L0_ffn_shared_up_w"]),
        "dwT": _ext_d(inp["L0_ffn_down_w"], inp["L0_ffn_shared_down_w"]),
        "cvec": cv,
        "idn": np.eye(128, dtype=np.float32),
        "ohot": ohot,
    }
    return m


_PROG_CACHE = {}


def _program():
    if "p" not in _PROG_CACHE:
        _PROG_CACHE["p"] = build(include_ffn=True)
    return _PROG_CACHE["p"]


def kernel(**inputs):
    p = _program()
    shared = make_inputs(inputs, 0)
    x = np.asarray(inputs["L0_attn_x"], np.float32).reshape(8 * TCORE, D)
    in_maps = []
    for core in range(8):
        m = dict(shared)
        m["xT"] = (shared["xT"] if core == 0 else
                   np.ascontiguousarray(x[core * TCORE:(core + 1) * TCORE, :].T))
        in_maps.append(m)
    res = run_bass_kernel_spmd(p.nc, in_maps, core_ids=list(range(8)))
    out = np.empty((8 * TCORE, D), np.float32)
    for core in range(8):
        out[core * TCORE:(core + 1) * TCORE, :] = res.results[core]["out"].T
    return out.reshape(2, 2 * 8192, D)


# version 10
# speedup vs baseline: 1.1402x; 1.1402x over previous
import contextlib
import numpy as np
import concourse.bass as bass
import concourse.mybir as mybir
from concourse.bass_utils import run_bass_kernel_spmd

F32 = mybir.dt.float32
BF16 = mybir.dt.bfloat16
U32 = mybir.dt.uint32
AX = mybir.AluOpType
AF = mybir.ActivationFunctionType
XAX = mybir.AxisListType.X

D = 1024
R = 8
TCORE = 4096
CA = 512
NCHA = TCORE // CA
NE = 32
IE = 512
G = 4
EPS1 = 1e-5
EPS2 = 1e-6

ENGS = ("sync", "gpsimd", "tensor", "vector", "scalar")

CV_LN1W, CV_LN1B, CV_LNCW, CV_LNCB, CV_LNGW, CV_LNGB, CV_BG = 0, 8, 16, 24, 32, 40, 48
CV_RBIAS = 56
CV_EGRP = 57
CV_EIOTA = 89
CV_GTINY = 121
CV_ETINY = 125
CV_NCOLS = 157


class P:
    def __init__(self):
        self.nc = bass.Bass("TRN2", target_bir_lowering=False)
        nc = self.nc
        dram = lambda n, s, dt=F32, k="ExternalInput": nc.dram_tensor(n, s, dt, kind=k).ap()
        self.xT = dram("xT", [D, TCORE])
        self.wt = dram("wt", [D, D])
        self.wo = dram("wo", [D, D])
        self.wg = dram("wg", [2 * D, D])
        self.rwT = dram("rwT", [D, NE])
        self.gwT = dram("gwT", [NE + 2, D, IE])
        self.uwT = dram("uwT", [NE + 2, D, IE])
        self.dwT = dram("dwT", [NE + 2, IE, D])
        self.cvec = dram("cvec", [128, CV_NCOLS])
        self.idn = dram("idn", [128, 128])
        self.ohot = dram("ohot", [NE, NE * 128], BF16)
        self.out = dram("out", [D, TCORE], k="ExternalOutput")
        self.x1h = dram("x1h", [D, TCORE], k="Internal")

    def beg(self, E):
        self.E = E
        self.sem_n = dict.fromkeys(self.sems, 0)
        self.wtgt = {}

    def new_sems(self, ctx, names):
        self.sems = {}
        for n in names:
            self.sems[n] = ctx.enter_context(self.nc.semaphore(n))

    def inc(self, sem, k, instr=None):
        self.sem_n[sem] += k
        if instr is not None:
            instr.then_inc(self.sems[sem], k)
        return self.sem_n[sem]

    def wait(self, eng, owner, sem, val=None):
        if self.E == owner:
            eng.wait_ge(self.sems[sem], self.sem_n[sem] if val is None else val)

    def on(self, owner):
        return self.E == owner

    def n(self, sem):
        return self.sem_n[sem]


def emit_setup(p, eng, t):
    if p.on("sync"):
        i1 = eng.dma_start(t["cvec"][:, :], p.cvec)
    else:
        i1 = None
    p.inc("s_ld", 16, i1)
    if p.on("gpsimd"):
        for r in range(R):
            ji = eng.iota(t["hcol8"][:, r:r + 1], [[0, 1]], base=128 * r,
                          channel_multiplier=1, allow_small_or_imprecise_dtypes=True)
    else:
        ji = None
    p.inc("s_gp", 1, ji)

    i = None
    if p.on("vector"):
        eng.wait_ge(p.sems["s_gp"], 1)
        eng.memset(t["onesr"][:, :], 1.0)
        eng.memset(t["onesc"][:, :], 1.0)
        eng.memset(t["csc"][:, 0:1], 1023.0)
        eng.memset(t["csc"][:, 1:2], 0.5)
        eng.drain()
        eng.reciprocal(t["csc"][:, 2:3], t["csc"][:, 0:1])
        eng.drain()
        eng.tensor_scalar(t["l8"][:, :], t["hcol8"][:, :], 6.0, None, AX.mult)
        eng.drain()
        i = eng.tensor_scalar(t["l8"][:, :], t["l8"][:, :], t["csc"][:, 2:3], 2.0,
                              AX.mult, AX.add)
    p.inc("s_v", 1, i)
    i = None
    if p.on("scalar"):
        eng.wait_ge(p.sems["s_v"], p.n("s_v"))
        i = eng.activation(t["csc"][:, 3:4], t["csc"][:, 1:2], AF.Ln)
    p.inc("s_sc", 1, i)
    i = None
    if p.on("vector"):
        eng.wait_ge(p.sems["s_sc"], p.n("s_sc"))
        eng.reciprocal(t["rl8"][:, :], t["l8"][:, :])
        eng.drain()
        i = eng.tensor_scalar(t["lna8"][:, :], t["rl8"][:, :], t["csc"][:, 3:4],
                              None, AX.mult)
    p.inc("s_v", 1, i)
    i = None
    if p.on("scalar"):
        eng.wait_ge(p.sems["s_v"], p.n("s_v"))
        i = eng.activation(t["a8"][:, :], t["lna8"][:, :], AF.Exp)
    p.inc("s_sc", 1, i)
    i = None
    if p.on("vector"):
        eng.wait_ge(p.sems["s_sc"], p.n("s_sc"))
        i = eng.tensor_scalar(t["b8"][:, :], t["a8"][:, :], -1.0, 1.0, AX.mult, AX.add)
        eng.drain()
    p.inc("s_v", 1, i)


def emit_ln(p, eng, t, src, mode, eps_bias, wcol, bcol, b0, b1, sqn,
            src_v=None, src_sc=None, bank_free_v=None, res_war=None,
            out_a=None, out_b=None):
    sem = p.sems
    ps = t["ps"]
    sigmoid = mode.endswith("sig")
    b_pe = p.n("s_pe")
    b_sc = p.n("s_sc")
    for r in range(R):
        sp = r % 2
        sq = t[sqn[sp]]
        i = None
        if p.on("scalar"):
            if r == 0:
                if src_v is not None:
                    eng.wait_ge(sem["s_v"], src_v)
                if src_sc is not None and src_sc > 0:
                    eng.wait_ge(sem["s_sc"], src_sc)
            if r >= 2:
                eng.wait_ge(sem["s_pe"], b_pe + (r - 2) + 1)
            i = eng.activation(sq[:, :], src[:, r * CA:(r + 1) * CA], AF.Square)
        p.inc("s_sc", 1, i)
        i = None
        if p.on("tensor"):
            if r == 0:
                if bank_free_v is not None:
                    eng.wait_ge(sem["s_v"], bank_free_v)
                if src_v is not None:
                    eng.wait_ge(sem["s_v"], src_v)
                if src_sc is not None and src_sc > 0:
                    eng.wait_ge(sem["s_sc"], src_sc)
            eng.matmul(ps[b0][0:1, :], t["onesc"][:, 0:1], src[:, r * CA:(r + 1) * CA],
                       start=(r == 0), stop=(r == R - 1))
            eng.wait_ge(sem["s_sc"], b_sc + r + 1)
            i = eng.matmul(ps[b1][0:1, :], t["onesc"][:, 0:1], sq[:, :],
                           start=(r == 0), stop=(r == R - 1))
        p.inc("s_pe", 1, i)
    stats_pe = p.n("s_pe")
    i = None
    if p.on("vector"):
        eng.wait_ge(sem["s_pe"], stats_pe)
        st = t["strow"]
        eng.tensor_scalar(st[0:1, 0:CA], ps[b0][0:1, :], 1.0 / D, None, AX.mult)
        eng.tensor_scalar(st[0:1, CA:2 * CA], ps[b1][0:1, :], 1.0 / D, None, AX.mult)
        eng.drain()
        eng.tensor_tensor(st[0:1, 2 * CA:3 * CA], st[0:1, 0:CA], st[0:1, 0:CA], AX.mult)
        eng.drain()
        eng.tensor_tensor(st[0:1, CA:2 * CA], st[0:1, CA:2 * CA],
                          st[0:1, 2 * CA:3 * CA], AX.subtract)
        eng.drain()
        if mode.startswith("fused1"):
            i = eng.tensor_scalar(st[0:1, CA:2 * CA], st[0:1, CA:2 * CA],
                                  1.0 + EPS2, EPS1 * EPS2, AX.mult, AX.add)
        else:
            i = eng.tensor_scalar(st[0:1, CA:2 * CA], st[0:1, CA:2 * CA],
                                  1.0, eps_bias, AX.mult, AX.add)
    p.inc("s_v", 1, i)
    i = None
    if p.on("scalar"):
        eng.wait_ge(sem["s_v"], p.n("s_v"))
        eng.activation(t["strow"][0:1, 2 * CA:3 * CA], t["strow"][0:1, CA:2 * CA], AF.Ln)
        eng.drain()
        i = eng.activation(t["strow"][0:1, CA:2 * CA],
                           t["strow"][0:1, 2 * CA:3 * CA], AF.Exp, scale=-0.5)
    p.inc("s_sc", 1, i)
    i = None
    if p.on("vector"):
        eng.wait_ge(sem["s_sc"], p.n("s_sc"))
        i = eng.tensor_tensor(t["strow"][0:1, 0:CA], t["strow"][0:1, 0:CA],
                              t["strow"][0:1, CA:2 * CA], AX.mult)
    p.inc("s_v", 1, i)
    i = None
    if p.on("tensor"):
        eng.wait_ge(sem["s_v"], p.n("s_v"))
        eng.matmul(ps[b0][:, :], t["onesr"][0:1, :], t["strow"][0:1, CA:2 * CA],
                   start=True, stop=True)
        i = eng.matmul(ps[b1][:, :], t["onesr"][0:1, :], t["strow"][0:1, 0:CA],
                       start=True, stop=True)
    p.inc("s_pe", 1, i)
    bc_pe = p.n("s_pe")
    b_sc2 = p.n("s_sc")
    spr = 2 if out_b is not None else 1
    for r in range(R):
        c0 = r * CA
        sp = r % 2
        sq = t[sqn[sp]]
        i = None
        if p.on("vector"):
            eng.wait_ge(sem["s_pe"], bc_pe)
            if r == 0 and res_war is not None:
                eng.wait_ge(sem[res_war[0]], res_war[1])
            if r >= 2 and not sigmoid:
                eng.wait_ge(sem["s_sc"], b_sc2 + (r - 2) * spr + 1)
            if r >= 2 and sigmoid:
                eng.wait_ge(sem["s_sc"], b_sc2 + (r - 2) + 1)
            eng.tensor_tensor(sq[:, :], src[:, c0:c0 + CA], ps[b0][:, :], AX.mult)
            eng.drain()
            i = eng.tensor_tensor(sq[:, :], sq[:, :], ps[b1][:, :], AX.subtract)
            eng.drain()
        p.inc("s_v", 1, i)
        i = None
        if p.on("scalar"):
            eng.wait_ge(sem["s_v"], p.n("s_v"))
            if sigmoid:
                i = eng.activation(out_a[:, c0:c0 + CA], sq[:, :], AF.Sigmoid,
                                   bias=t["cvec"][:, bcol + r:bcol + r + 1],
                                   scale=t["cvec"][:, wcol + r:wcol + r + 1])
            else:
                i = eng.activation(out_a[:, c0:c0 + CA], sq[:, :], AF.Identity,
                                   bias=t["cvec"][:, bcol + r:bcol + r + 1],
                                   scale=t["cvec"][:, wcol + r:wcol + r + 1])
        p.inc("s_sc", 1, i)
        if out_b is not None:
            i = None
            if p.on("scalar"):
                eng.drain()
                i = eng.copy(out_b[:, c0:c0 + CA], out_a[:, c0:c0 + CA])
            p.inc("s_sc", 1, i)
    return {"apply_v": p.n("s_v"), "out_sc": p.n("s_sc")}


def emit_phase_a(p, eng, t):
    sem = p.sems
    ps = t["ps"]

    def w_sl(wsb, kb, mi):
        return wsb[:, kb * D + mi * 128: kb * D + mi * 128 + 128]

    if p.on("gpsimd"):
        j1 = j2 = j3 = None
        for name, src in (("wt_sb", p.wt), ("wo_sb", p.wo)):
            sap = bass.AP(src.tensor, 0, [[D, 128], [128 * D, R], [1, D]])
            dap = bass.AP(t[name], 0, [[R * D, 128], [D, R], [1, D]])
            j = eng.dma_start(dap, sap)
            if name == "wt_sb":
                j1 = j
            else:
                j2 = j
        sap = bass.AP(p.wg.tensor, 0, [[D, 128], [128 * D, 2 * R], [1, D]])
        dap = bass.AP(t["wg_sb"], 0, [[2 * R * D, 128], [D, 2 * R], [1, D]])
        j3 = eng.dma_start(dap, sap)
    else:
        j1 = j2 = j3 = None
    p.inc("s_wA", 16, j1)
    p.inc("s_wA", 16, j2)
    p.inc("s_wA", 16, j3)
    if p.on("tensor"):
        eng.wait_ge(sem["s_wA"], p.n("s_wA"))
    if p.on("scalar") or p.on("vector"):
        eng.wait_ge(sem["s_ld"], 16)

    def proj(wsb, rhs_of, out_evac, nk, rdy_v, rdy_sc, bank_free, evac_eng,
             evac_wait=None):
        esem = "s_v" if evac_eng == "vector" else "s_sc"
        b_e = p.n(esem)
        for mi in range(R):
            bank = mi % 2
            i = None
            if p.on("tensor"):
                if mi == 0:
                    if rdy_v:
                        eng.wait_ge(sem["s_v"], rdy_v)
                    if rdy_sc:
                        eng.wait_ge(sem["s_sc"], rdy_sc)
                    if bank_free:
                        eng.wait_ge(sem[bank_free[0]], bank_free[1])
                if mi >= 2:
                    eng.wait_ge(sem[esem], b_e + (mi - 2) + 1)
                for kb in range(nk):
                    i = eng.matmul(ps[bank][:, :], w_sl(wsb, kb, mi), rhs_of(kb),
                                   start=(kb == 0), stop=(kb == nk - 1))
            p.inc("s_pe", 1, i)
            i = None
            if p.on(evac_eng):
                if mi == 0 and evac_wait is not None:
                    eng.wait_ge(sem[evac_wait[0]], evac_wait[1])
                eng.wait_ge(sem["s_pe"], p.n("s_pe"))
                i = out_evac(eng, mi, ps[bank])
            p.inc(esem, 1, i)
        return {"evac": p.n(esem), "esem": esem}

    ln1_recs = [None] * (NCHA + 1)
    ln1_apply_prev = 0
    scan_v = 0
    lnc_apply_prev = 0
    lng_apply_prev = 0
    proj_evac_prev = None
    mix_resfree = 0
    mix_done = 0

    def emit_ln1(j, res_war_val):
        par = j % 2
        xq = t["xq0"] if par == 0 else t["xq1"]
        if j >= 2:
            p.wait(eng, "sync", "s_x1s_%d" % par)
        i = None
        if p.on("sync"):
            sap = bass.AP(p.xT.tensor, j * CA, [[TCORE, 128], [128 * TCORE, R], [1, CA]])
            dap = bass.AP(xq, 0, [[R * CA, 128], [CA, R], [1, CA]])
            i = eng.dma_start(dap, sap)
        p.inc("s_x_%d" % par, 16, i)
        if p.on("scalar") or p.on("tensor") or p.on("vector"):
            eng.wait_ge(sem["s_x_%d" % par], p.n("s_x_%d" % par))
        return emit_ln(p, eng, t, src=xq, mode="fused1", eps_bias=0.0,
                       wcol=CV_LN1W, bcol=CV_LN1B, b0=2, b1=3, sqn=("sqA0", "sqA1"),
                       src_v=None, src_sc=None, bank_free_v=ln1_apply_prev,
                       res_war=("s_vx1a", res_war_val),
                       out_a=t["res"], out_b=t["resb"])

    ln1_recs[0] = emit_ln1(0, 0)

    for j in range(NCHA):
        par = j % 2
        xq = t["xq0"] if par == 0 else t["xq1"]

        def evac_u(eng, mi, bank):
            return eng.activation(t["u"][:, mi * CA:(mi + 1) * CA], bank[:, :],
                                  AF.Identity, scale=t["b8"][:, mi:mi + 1])
        pr = proj(t["wt_sb"], lambda kb: t["resb"][:, kb * CA:(kb + 1) * CA],
                  evac_u, R, rdy_v=None, rdy_sc=ln1_recs[j]["out_sc"],
                  bank_free=(proj_evac_prev["esem"], proj_evac_prev["evac"])
                  if proj_evac_prev else None, evac_eng="scalar",
                  evac_wait=("s_vx1", mix_done))
        u_rec = pr

        i = None
        if p.on("vector"):
            eng.wait_ge(sem["s_sc"], u_rec["evac"])
            for r in range(R):
                c0 = r * CA
                eng.drain()
                a_b = bass.AP(t["a8"], r, [[R, 128], [0, CA]])
                eng.tensor_tensor_scan(t["cx"][:, c0:c0 + CA],
                                       a_b, t["u"][:, c0:c0 + CA],
                                       (0.0 if j == 0 else t["carry"][:, r:r + 1]),
                                       AX.mult, AX.add)
                eng.drain()
                i = eng.tensor_copy(t["carry"][:, r:r + 1],
                                    t["cx"][:, c0 + CA - 1:c0 + CA])
            eng.drain()
        p.inc("s_v", 1, i)
        scan_v = p.n("s_v")

        lnc_rec = emit_ln(p, eng, t, src=t["cx"], mode="plain", eps_bias=EPS2,
                          wcol=CV_LNCW, bcol=CV_LNCB, b0=4, b1=5,
                          sqn=("sq0", "sq1"), src_v=scan_v, src_sc=None,
                          bank_free_v=lnc_apply_prev, res_war=None,
                          out_a=t["lncxb"])
        lnc_apply_prev = lnc_rec["apply_v"]

        def evac_o(eng, mi, bank):
            eng.copy(t["outp"][:, mi * CA:(mi + 1) * CA], bank[:, :])
            eng.drain()
            return eng.copy(t["outpb"][:, mi * CA:(mi + 1) * CA],
                            t["outp"][:, mi * CA:(mi + 1) * CA])
        pr = proj(t["wo_sb"], lambda kb: t["lncxb"][:, kb * CA:(kb + 1) * CA],
                  evac_o, R, rdy_v=None, rdy_sc=lnc_rec["out_sc"],
                  bank_free=("s_sc", u_rec["evac"]), evac_eng="scalar")
        wo_rec = pr

        def evac_g(eng, mi, bank):
            return eng.activation(t["u"][:, mi * CA:(mi + 1) * CA], bank[:, :],
                                  AF.Identity,
                                  bias=t["cvec"][:, CV_BG + mi:CV_BG + mi + 1])
        pr = proj(t["wg_sb"],
                  lambda kb: (t["resb"][:, kb * CA:(kb + 1) * CA] if kb < R
                              else t["outpb"][:, (kb - R) * CA:(kb - R + 1) * CA]),
                  evac_g, 2 * R, rdy_v=None, rdy_sc=wo_rec["evac"],
                  bank_free=("s_sc", wo_rec["evac"]), evac_eng="scalar")
        wg_rec = pr
        proj_evac_prev = wg_rec

        lng_rec = emit_ln(p, eng, t, src=t["u"], mode="plain_sig", eps_bias=EPS2,
                          wcol=CV_LNGW, bcol=CV_LNGB, b0=6, b1=7,
                          sqn=("mx0", "mx1"), src_v=None, src_sc=wg_rec["evac"],
                          bank_free_v=lng_apply_prev,
                          res_war=("s_vx1", mix_done), out_a=t["u"])
        lng_apply_prev = lng_rec["apply_v"]

        if j + 1 < NCHA:
            ln1_recs[j + 1] = emit_ln1(j + 1, p.n("s_vx1a") + R)
            ln1_apply_prev = ln1_recs[j + 1]["apply_v"]

        b_sc = lng_rec["out_sc"]
        for r in range(R):
            c0 = r * CA
            sp = r % 2
            sq = t["mg0" if sp == 0 else "mg1"]
            i = i2 = None
            if p.on("gpsimd"):
                eng.wait_ge(sem["s_sc"], b_sc)
                eng.tensor_tensor(sq[:, :], t["outp"][:, c0:c0 + CA],
                                  t["res"][:, c0:c0 + CA], AX.subtract)
                i2 = eng.tensor_tensor(xq[:, c0:c0 + CA], xq[:, c0:c0 + CA],
                                       t["res"][:, c0:c0 + CA], AX.add)
                eng.drain()
                eng.tensor_tensor(sq[:, :], sq[:, :], t["u"][:, c0:c0 + CA], AX.mult)
                eng.drain()
                i = eng.tensor_tensor(xq[:, c0:c0 + CA], xq[:, c0:c0 + CA],
                                      sq[:, :], AX.add)
            p.inc("s_vx1a", 1, i2)
            p.inc("s_vx1", 1, i)
        mix_resfree = p.n("s_vx1a")
        mix_done = p.n("s_vx1")
        p.wait(eng, "sync", "s_vx1")
        i = None
        if p.on("sync"):
            sap = bass.AP(xq, 0, [[R * CA, 128], [CA, R], [1, CA]])
            dap = bass.AP(p.x1h.tensor, j * CA, [[TCORE, 128], [128 * TCORE, R], [1, CA]])
            i = eng.dma_start(dap, sap)
        p.inc("s_x1s_%d" % par, 16, i)
    for par in range(2):
        p.wait(eng, "sync", "s_x1s_%d" % par)


TQ = 1024
NQ = TCORE // TQ
CT = 512
NTC = TQ // CT
NEX = NE + 2
NGRP = NEX // 2
NTT = TQ // 128


def grp_experts(g):
    return (NE, NE + 1) if g == 0 else (2 * (g - 1), 2 * (g - 1) + 1)


def emit_phase_b(p, eng, t):
    sem = p.sems
    ps = t["ps"]

    def mtile(buf, r, tc):
        return buf[:, r * TQ + tc * CT: r * TQ + tc * CT + CT]

    i1 = i2 = i3 = None
    if p.on("sync"):
        i1 = eng.dma_start(t["idn"][:, :], p.idn)
        i2 = eng.dma_start(t["ohot"][:, :], p.ohot)
        sap = bass.AP(p.rwT.tensor, 0, [[NE, 128], [128 * NE, R], [1, NE]])
        dap = bass.AP(t["rw_sb"], 0, [[R * NE, 128], [NE, R], [1, NE]])
        i3 = eng.dma_start(dap, sap)
    p.inc("s_ld", 16, i1)
    p.inc("s_ld", 16, i2)
    p.inc("s_ld", 16, i3)
    if p.on("tensor") or p.on("vector"):
        eng.wait_ge(sem["s_ld"], p.n("s_ld"))

    for q in range(NQ):
        Q0 = q * TQ
        q_fwg = [p.n("s_fwg0"), p.n("s_fwg1")]
        q_fwd = p.n("s_fwd")

        def load_grp(g):
            es = grp_experts(g)
            for el in range(2):
                e = es[el]
                par = el
                gw = t["gw0"] if par == 0 else t["gw1"]
                uw = t["uw0"] if par == 0 else t["uw1"]
                slot = (2 * g + el) % 4
                iA = iB = iC = None
                if p.on("gpsimd"):
                    if g >= 1 or q > 0:
                        eng.wait_ge(sem["s_fwg%d" % par], q_fwg[par] + g)
                    if g >= 2:
                        eng.wait_ge(sem["s_fwd"], q_fwd + g - 1)
                    elif q > 0:
                        eng.wait_ge(sem["s_fwd"], q_fwd)
                    sap = bass.AP(p.gwT.tensor, e * D * IE,
                                  [[IE, 128], [128 * IE, R], [1, IE]])
                    dap = bass.AP(gw, 0, [[R * IE, 128], [IE, R], [1, IE]])
                    iA = eng.dma_start(dap, sap)
                    sap = bass.AP(p.uwT.tensor, e * D * IE,
                                  [[IE, 128], [128 * IE, R], [1, IE]])
                    dap = bass.AP(uw, 0, [[R * IE, 128], [IE, R], [1, IE]])
                    iB = eng.dma_start(dap, sap)
                    sap = bass.AP(p.dwT.tensor, e * IE * D,
                                  [[D, 128], [128 * D, 4], [1, D]])
                    dap = bass.AP(t["dw"], slot * 4 * D,
                                  [[16 * D, 128], [D, 4], [1, D]])
                    iC = eng.dma_start(dap, sap)
                p.inc("s_wg%d" % par, 16, iA)
                p.inc("s_wu%d" % par, 16, iB)
                p.inc("s_wd%d" % slot, 16, iC)
                p.wtgt[(q, g, "g", par)] = p.n("s_wg%d" % par)
                p.wtgt[(q, g, "u", par)] = p.n("s_wu%d" % par)
                p.wtgt[(q, g, "d", slot)] = p.n("s_wd%d" % slot)

        for tcl in range(NTC):
            T0 = Q0 + tcl * CT
            p.wait(eng, "sync", "s_v")
            p.wait(eng, "sync", "s_vq")
            p.wait(eng, "sync", "s_outs")
            i = None
            if p.on("sync"):
                sap = bass.AP(p.x1h.tensor, T0, [[TCORE, 128], [128 * TCORE, R], [1, CT]])
                dap = bass.AP(t["x1t"], 0, [[R * CT, 128], [CT, R], [1, CT]])
                i = eng.dma_start(dap, sap)
            p.inc("s_x1r", 16, i)
            if p.on("scalar") or p.on("tensor") or p.on("vector"):
                eng.wait_ge(sem["s_x1r"], p.n("s_x1r"))
            b_pe = p.n("s_pe")
            b_sc = p.n("s_sc")
            for r in range(R):
                sp = r % 2
                sq = t["sq0" if sp == 0 else "sq1"]
                i = None
                if p.on("scalar"):
                    if r == 0:
                        eng.wait_ge(sem["s_v"], p.n("s_v"))
                    if r >= 2:
                        eng.wait_ge(sem["s_pe"], b_pe + (r - 2) + 1)
                    i = eng.activation(sq[:, :], t["x1t"][:, r * CT:(r + 1) * CT],
                                       AF.Square)
                p.inc("s_sc", 1, i)
                if r == 0:
                    p.wait(eng, "tensor", "s_v")
                i = None
                if p.on("tensor"):
                    eng.matmul(ps[4][0:1, :], t["onesc"][:, 0:1],
                               t["x1t"][:, r * CT:(r + 1) * CT],
                               start=(r == 0), stop=(r == R - 1))
                    eng.wait_ge(sem["s_sc"], b_sc + r + 1)
                    i = eng.matmul(ps[5][0:1, :], t["onesc"][:, 0:1], sq[:, :],
                                   start=(r == 0), stop=(r == R - 1))
                p.inc("s_pe", 1, i)
            i = None
            if p.on("vector"):
                eng.wait_ge(sem["s_pe"], p.n("s_pe"))
                st = t["strowB"]
                eng.tensor_scalar(st[0:1, 0:CT], ps[4][0:1, :], 1.0 / D, None, AX.mult)
                eng.tensor_scalar(st[0:1, CT:2 * CT], ps[5][0:1, :], 1.0 / D,
                                  None, AX.mult)
                eng.drain()
                eng.tensor_tensor(st[0:1, 2 * CT:3 * CT], st[0:1, 0:CT],
                                  st[0:1, 0:CT], AX.mult)
                eng.drain()
                eng.tensor_tensor(st[0:1, CT:2 * CT], st[0:1, CT:2 * CT],
                                  st[0:1, 2 * CT:3 * CT], AX.subtract)
                eng.drain()
                i = eng.tensor_scalar(st[0:1, CT:2 * CT], st[0:1, CT:2 * CT],
                                      1.0, EPS1, AX.mult, AX.add)
            p.inc("s_v", 1, i)
            i = None
            if p.on("scalar"):
                eng.wait_ge(sem["s_v"], p.n("s_v"))
                eng.activation(t["strowB"][0:1, 2 * CT:3 * CT],
                               t["strowB"][0:1, CT:2 * CT], AF.Ln)
                eng.drain()
                i = eng.activation(t["strowB"][0:1, CT:2 * CT],
                                   t["strowB"][0:1, 2 * CT:3 * CT], AF.Exp, scale=-0.5)
            p.inc("s_sc", 1, i)
            i = None
            if p.on("vector"):
                eng.wait_ge(sem["s_sc"], p.n("s_sc"))
                i = eng.tensor_tensor(t["strowB"][0:1, 0:CT], t["strowB"][0:1, 0:CT],
                                      t["strowB"][0:1, CT:2 * CT], AX.mult)
            p.inc("s_v", 1, i)
            i = None
            if p.on("tensor"):
                eng.wait_ge(sem["s_v"], p.n("s_v"))
                eng.wait_ge(sem["s_v2"], p.n("s_v2"))
                eng.matmul(ps[6][:, :], t["onesr"][0:1, :],
                           t["strowB"][0:1, CT:2 * CT], start=True, stop=True)
                i = eng.matmul(ps[7][:, :], t["onesr"][0:1, :],
                               t["strowB"][0:1, 0:CT], start=True, stop=True)
            p.inc("s_pe", 1, i)
            b_pe2 = p.n("s_pe")
            for r in range(R):
                sp = r % 2
                sq = t["sq0" if sp == 0 else "sq1"]
                i = None
                if p.on("vector"):
                    eng.wait_ge(sem["s_pe"], b_pe2 if r < 2 else b_pe2 + (r - 2) + 1)
                    eng.tensor_tensor(sq[:, :], t["x1t"][:, r * CT:(r + 1) * CT],
                                      ps[6][:, :], AX.mult)
                    eng.drain()
                    eng.tensor_tensor(sq[:, :], sq[:, :], ps[7][:, :], AX.subtract)
                    eng.drain()
                    i = eng.tensor_copy(mtile(t["x2b"], r, tcl), sq[:, :])
                    eng.drain()
                p.inc("s_v", 1, i)
                if r == R - 1:
                    d = None
                    if p.on("vector"):
                        d = eng.drain()
                    p.inc("s_v2", 1, d)
                i = None
                if p.on("tensor"):
                    eng.wait_ge(sem["s_v"], p.n("s_v"))
                    i = eng.matmul(ps[3][0:NE, :], t["rw_sb"][:, r * NE:(r + 1) * NE],
                                   sq[:, :], start=(r == 0), stop=(r == R - 1))
                p.inc("s_pe", 1, i)
            i = None
            if p.on("scalar"):
                eng.wait_ge(sem["s_pe"], p.n("s_pe"))
                i = eng.activation(t["scores"][0:NE, tcl * CT:(tcl + 1) * CT],
                                   ps[3][0:NE, :], AF.Sigmoid)
            p.inc("s_sc", 1, i)
            i = None
            if p.on("vector"):
                eng.wait_ge(sem["s_sc"], p.n("s_sc"))
                i = eng.tensor_scalar(t["sfc"][0:NE, tcl * CT:(tcl + 1) * CT],
                                      t["scores"][0:NE, tcl * CT:(tcl + 1) * CT],
                                      t["cvec"][0:NE, CV_RBIAS:CV_RBIAS + 1],
                                      None, AX.add)
            p.inc("s_v", 1, i)
        v_b1_done = p.n("s_v")
        sc_b1_done = p.n("s_sc")

        load_grp(0)

        for tt in range(NTT):
            c0 = tt * 128
            i = None
            if p.on("tensor"):
                eng.wait_ge(sem["s_v"], v_b1_done)
                eng.wait_ge(sem["s_sc"], sc_b1_done)
                if tt >= 1:
                    eng.wait_ge(sem["s_v3"], p.n("s_v3"))
                eng.transpose(ps[4][:, 0:NE], t["sfc"][0:NE, c0:c0 + 128],
                              t["idn"][0:NE, 0:NE])
                i = eng.transpose(ps[5][:, 0:NE], t["scores"][0:NE, c0:c0 + 128],
                                  t["idn"][0:NE, 0:NE])
            p.inc("s_pe", 1, i)
            i = None
            if p.on("vector"):
                eng.wait_ge(sem["s_pe"], p.n("s_pe"))
                eng.tensor_copy(t["sfc_tm"][:, tt * 32:(tt + 1) * 32], ps[4][:, 0:NE])
                eng.drain()
                i = eng.tensor_copy(t["sc_tm"][:, tt * 32:(tt + 1) * 32],
                                    ps[5][:, 0:NE])
                eng.drain()
            p.inc("s_v3", 1, i)
        trv = []
        for tt in range(NTT):
            sfct = t["sfc_tm"][:, tt * 32:(tt + 1) * 32]
            sct = t["sc_tm"][:, tt * 32:(tt + 1) * 32]
            w32t = t["w32tm"][:, tt * 32:(tt + 1) * 32]
            i = None
            if p.on("vector"):
                for gi in range(4):
                    eng.max(t["m8"][:, :], sfct[:, gi * 8:(gi + 1) * 8])
                    eng.drain()
                    eng.tensor_tensor(t["gs8"][:, gi:gi + 1], t["m8"][:, 0:1],
                                      t["m8"][:, 1:2], AX.add)
                    eng.drain()
                eng.tensor_tensor(t["gs8"][:, 0:4], t["gs8"][:, 0:4],
                                  t["cvec"][:, CV_GTINY:CV_GTINY + 4], AX.add)
                eng.drain()
                eng.max(t["m8"][:, :], t["gs8"][:, :])
                eng.drain()
                eng.tensor_scalar(t["gm4"][:, :], t["gs8"][:, 0:4],
                                  t["m8"][:, 1:2], None, AX.is_ge)
                eng.drain()
                for gi in range(4):
                    eng.tensor_scalar(t["tmp32"][:, gi * 8:(gi + 1) * 8],
                                      sfct[:, gi * 8:(gi + 1) * 8],
                                      t["gm4"][:, gi:gi + 1], None, AX.mult)
                eng.drain()
                eng.tensor_tensor(t["tmp32"][:, :], t["tmp32"][:, :],
                                  t["cvec"][:, CV_ETINY:CV_ETINY + 32], AX.add)
                eng.drain()
                eng.max(t["m8"][:, :], t["tmp32"][:, :])
                eng.drain()
                eng.tensor_scalar(t["eqm"][:, :], t["tmp32"][:, :],
                                  t["m8"][:, 3:4], None, AX.is_ge)
                eng.drain()
                eng.tensor_tensor(t["eqm"][:, :], t["eqm"][:, :], sct, AX.mult)
                eng.drain()
                eng.reduce_sum(t["skc"][:, 0:1], t["eqm"][:, :],
                               axis=mybir.AxisListType.X)
                eng.drain()
                eng.tensor_scalar(t["skc"][:, 1:2], t["skc"][:, 0:1], 1e-20,
                                  None, AX.add)
                eng.drain()
                eng.reciprocal(t["skc"][:, 2:3], t["skc"][:, 1:2])
                eng.drain()
                eng.tensor_scalar(t["skc"][:, 2:3], t["skc"][:, 2:3], 2.5,
                                  None, AX.mult)
                eng.drain()
                eng.tensor_scalar(w32t, t["eqm"][:, :], t["skc"][:, 2:3],
                                  None, AX.mult)
                eng.drain()
                i = eng.drain()
            p.inc("s_v", 1, i)
            trv.append(p.n("s_v"))

        set0_free = v_b1_done
        set1_free = None
        s_v3_all = p.n("s_v3")
        for g in range(NGRP):
            pseudo = (g == 0)
            if g + 1 <= NGRP - 1:
                load_grp(g + 1)
            its = [(0, 0), (1, 0), (0, 1), (1, 1)]
            h2_v = [None] * 4
            for it, (el, tc) in enumerate(its):
                bs = (it % 2) * 4
                par = el
                gw = t["gw0"] if par == 0 else t["gw1"]
                uw = t["uw0"] if par == 0 else t["uw1"]
                sg = t["sg0"] if par == 0 else t["sg1"]
                b_gate = p.n("s_pe")
                for mi in range(4):
                    i = None
                    if p.on("tensor"):
                        if mi == 0:
                            if it == 0:
                                eng.wait_ge(sem["s_v"], set0_free)
                            elif it == 1:
                                if pseudo:
                                    eng.wait_ge(sem["s_v"], v_b1_done)
                                    eng.wait_ge(sem["s_v3"], s_v3_all)
                                else:
                                    eng.wait_ge(sem["s_v"], set1_free)
                            else:
                                eng.wait_ge(sem["s_v"], h2_v[it - 2])
                            eng.wait_ge(sem["s_wg%d" % par],
                                        p.wtgt[(q, g, "g", par)])
                            if tc == 0:
                                eng.wait_ge(sem["s_wu%d" % par],
                                            p.wtgt[(q, g, "u", par)])
                            if g == 0 and it == 0:
                                eng.wait_ge(sem["s_sc"], sc_b1_done)
                        for kb in range(R):
                            i = eng.matmul(
                                ps[bs + mi][:, :],
                                gw[:, kb * IE + mi * 128: kb * IE + mi * 128 + 128],
                                mtile(t["x2b"], kb, tc),
                                start=(kb == 0), stop=(kb == R - 1))
                    p.inc("s_pe", 1, i)
                b_sc2 = p.n("s_sc2")
                for mi in range(4):
                    i = None
                    if p.on("scalar"):
                        eng.wait_ge(sem["s_pe"], b_gate + mi + 1)
                        i = eng.activation(sg[:, mi * CT:(mi + 1) * CT],
                                           ps[bs + mi][:, :], AF.Silu)
                    p.inc("s_sc2", 1, i)
                b_up = p.n("s_pe")
                for mi in range(4):
                    i = None
                    if p.on("tensor"):
                        eng.wait_ge(sem["s_sc2"], b_sc2 + mi + 1)
                        for kb in range(R):
                            i = eng.matmul(
                                ps[bs + mi][:, :],
                                uw[:, kb * IE + mi * 128: kb * IE + mi * 128 + 128],
                                mtile(t["x2b"], kb, tc),
                                start=(kb == 0), stop=(kb == R - 1))
                    p.inc("s_pe", 1, i)
                for mi in range(4):
                    hsl = t["hidden"][:, (tc * 8 + el * 4 + mi) * CT:
                                      (tc * 8 + el * 4 + mi + 1) * CT]
                    i = None
                    if p.on("vector"):
                        eng.wait_ge(sem["s_pe"], b_up + mi + 1)
                        eng.wait_ge(sem["s_sc2"], p.n("s_sc2"))
                        eng.tensor_tensor(hsl, sg[:, mi * CT:(mi + 1) * CT],
                                          ps[bs + mi][:, :], AX.mult)
                        eng.drain()
                        if not pseudo:
                            idx = el * 2 + tc
                            eng.tensor_tensor(
                                hsl, hsl,
                                t["wbc_sb"][:, idx * CT:(idx + 1) * CT], AX.mult)
                            eng.drain()
                        i = eng.drain()
                    p.inc("s_v", 1, i)
                h2_v[it] = p.n("s_v")
                if tc == 1:
                    d = None
                    if p.on("vector"):
                        d = eng.drain()
                    p.inc("s_fwg%d" % par, 1, d)
            evac_v = {}
            for tc, half in ((0, 1), (0, 0), (1, 1), (1, 0)):
                if (tc, half) == (1, 0) and g + 1 <= NGRP - 1:
                    if g == 0:
                        for tt in range(NTT):
                            i = None
                            if p.on("tensor"):
                                eng.wait_ge(sem["s_v"], trv[tt])
                                if tt == 0:
                                    eng.wait_ge(sem["s_v"], evac_v[(0, 0)])
                                if tt >= 1:
                                    eng.wait_ge(sem["s_v4"], p.n("s_v4"))
                                i = eng.transpose(
                                    ps[6][0:NE, 0:128],
                                    t["w32tm"][:, tt * 32:(tt + 1) * 32],
                                    t["idn"][:, :])
                            p.inc("s_pe", 1, i)
                            i = None
                            if p.on("vector"):
                                eng.wait_ge(sem["s_pe"], p.n("s_pe"))
                                i = eng.tensor_copy(
                                    t["w32cm"][0:NE, tt * 128:(tt + 1) * 128],
                                    ps[6][0:NE, 0:128])
                                eng.drain()
                            p.inc("s_v4", 1, i)
                    for idx in range(4):
                        elw, tcw = idx // 2, idx % 2
                        e_r = 2 * g + elw
                        i = None
                        if p.on("tensor"):
                            if idx == 0:
                                eng.wait_ge(sem["s_v4"], p.n("s_v4"))
                                eng.wait_ge(sem["s_v"], evac_v[(0, 0)])
                            i = eng.matmul(
                                ps[4 + idx][:, :],
                                t["ohot"][0:NE, e_r * 128:(e_r + 1) * 128],
                                t["w32cm"][0:NE, tcw * CT:(tcw + 1) * CT],
                                start=True, stop=True)
                        p.inc("s_pe", 1, i)
                        i = None
                        if p.on("vector"):
                            eng.wait_ge(sem["s_pe"], p.n("s_pe"))
                            i = eng.tensor_copy(
                                t["wbc_sb"][:, idx * CT:(idx + 1) * CT],
                                ps[4 + idx][:, :])
                            eng.drain()
                        p.inc("s_v", 1, i)
                    wbc_next = p.n("s_v")
                bs = 0 if half == 1 else 4
                mos = range(half * 4, half * 4 + 4)
                for mo_i, mo in enumerate(mos):
                    ii = None
                    if p.on("tensor"):
                        if mo_i == 0:
                            eng.wait_ge(sem["s_wd%d" % ((2 * g) % 4)],
                                        p.wtgt[(q, g, "d", (2 * g) % 4)])
                            eng.wait_ge(sem["s_wd%d" % ((2 * g + 1) % 4)],
                                        p.wtgt[(q, g, "d", (2 * g + 1) % 4)])
                            if tc == 0 and half == 1:
                                eng.wait_ge(sem["s_v"], h2_v[2])
                            elif tc == 0 and half == 0:
                                eng.wait_ge(sem["s_v"], h2_v[3])
                            elif tc == 1 and half == 1:
                                eng.wait_ge(sem["s_v"], evac_v[(0, 1)])
                                eng.wait_ge(sem["s_v"], h2_v[3])
                            else:
                                eng.wait_ge(sem["s_v"], wbc_next
                                            if g + 1 <= NGRP - 1
                                            else evac_v[(0, 0)])
                        for el in range(2):
                            e_slot = (2 * g + el) % 4
                            for ki in range(4):
                                ii = eng.matmul(
                                    ps[bs + mo_i][:, :],
                                    t["dw"][:, e_slot * 4 * D + ki * D + mo * 128:
                                            e_slot * 4 * D + ki * D + mo * 128 + 128],
                                    t["hidden"][:, (tc * 8 + el * 4 + ki) * CT:
                                                (tc * 8 + el * 4 + ki + 1) * CT],
                                    start=(el == 0 and ki == 0),
                                    stop=(el == 1 and ki == 3))
                    p.inc("s_pe", 1, ii)
                i = None
                if p.on("vector"):
                    eng.wait_ge(sem["s_pe"], p.n("s_pe"))
                    for mo_i, mo in enumerate(mos):
                        if pseudo:
                            eng.tensor_copy(mtile(t["mixed"], mo, tc),
                                            ps[bs + mo_i][:, :])
                        else:
                            eng.tensor_tensor(mtile(t["mixed"], mo, tc),
                                              mtile(t["mixed"], mo, tc),
                                              ps[bs + mo_i][:, :], AX.add)
                        eng.drain()
                    i = eng.drain()
                p.inc("s_v", 1, i)
                evac_v[(tc, half)] = p.n("s_v")
            set0_free = evac_v[(1, 1)]
            set1_free = evac_v[(1, 0)]
            d = None
            if p.on("vector"):
                d = eng.drain()
            p.inc("s_fwd", 1, d)

        v_b4_done = p.n("s_v")
        for tcl in range(NTC):
            T0 = Q0 + tcl * CT
            p.wait(eng, "sync", "s_v", v_b4_done)
            p.wait(eng, "sync", "s_vq")
            p.wait(eng, "sync", "s_outs")
            i = None
            if p.on("sync"):
                sap = bass.AP(p.x1h.tensor, T0, [[TCORE, 128], [128 * TCORE, R], [1, CT]])
                dap = bass.AP(t["x1t"], 0, [[R * CT, 128], [CT, R], [1, CT]])
                i = eng.dma_start(dap, sap)
            p.inc("s_x1r", 16, i)
            i = None
            if p.on("vector"):
                eng.wait_ge(sem["s_x1r"], p.n("s_x1r"))
                for r in range(R):
                    eng.tensor_tensor(t["x1t"][:, r * CT:(r + 1) * CT],
                                      t["x1t"][:, r * CT:(r + 1) * CT],
                                      mtile(t["mixed"], r, tcl), AX.add)
                    eng.drain()
                i = eng.drain()
            p.inc("s_vq", 1, i)
            p.wait(eng, "sync", "s_vq")
            i = None
            if p.on("sync"):
                sap = bass.AP(t["x1t"], 0, [[R * CT, 128], [CT, R], [1, CT]])
                dap = bass.AP(p.out.tensor, T0, [[TCORE, 128], [128 * TCORE, R], [1, CT]])
                i = eng.dma_start(dap, sap)
            p.inc("s_outs", 16, i)
    p.wait(eng, "sync", "s_outs")

def build(include_ffn=False):
    p = P()
    nc = p.nc
    with contextlib.ExitStack() as ctx:
        sb = lambda n, s, dt=F32: ctx.enter_context(nc.sbuf_tensor(n, s, dt))
        t = {}
        t["cvec"] = sb("cvec_s", [128, CV_NCOLS])
        t["hcol8"] = sb("hcol8", [128, R])
        t["l8"] = sb("l8", [128, R])
        t["rl8"] = sb("rl8", [128, R])
        t["lna8"] = sb("lna8", [128, R])
        t["a8"] = sb("a8", [128, R])
        t["b8"] = sb("b8", [128, R])
        t["csc"] = sb("csc", [128, 4])
        t["onesr"] = sb("onesr", [1, 128])
        t["onesc"] = sb("onesc", [128, 1])
        t["carry"] = sb("carry", [128, R])
        sems = ["s_ld", "s_wA", "s_gp", "s_v", "s_v2", "s_v3", "s_v4", "s_sc",
                "s_sc2", "s_pe", "s_vx1", "s_x_0", "s_x_1", "s_x1s_0", "s_x1s_1",
                "s_out", "s_wg0", "s_wg1", "s_wu0", "s_wu1", "s_wd0", "s_wd1",
                "s_wd2", "s_wd3", "s_fwg0", "s_fwg1", "s_fwd",
                "s_x1r", "s_vq", "s_outs", "s_vx1a"]
        p.new_sems(ctx, sems)

        with contextlib.ExitStack() as c0:
            blk = c0.enter_context(nc.Block())
            for E in ENGS:
                p.beg(E)
                getattr(blk, E)(lambda eng: emit_setup(p, eng, t))
            counts = dict(p.sem_n)

        with contextlib.ExitStack() as c1:
            t["wt_sb"] = c1.enter_context(nc.sbuf_tensor("wt_sb", [128, R * D], BF16))
            t["wo_sb"] = c1.enter_context(nc.sbuf_tensor("wo_sb", [128, R * D], BF16))
            t["wg_sb"] = c1.enter_context(nc.sbuf_tensor("wg_sb", [128, 2 * R * D], BF16))
            for n, s, dt in (
                ("xq0", [128, R * CA], F32), ("xq1", [128, R * CA], F32),
                ("res", [128, R * CA], F32), ("resb", [128, R * CA], BF16),
                ("u", [128, R * CA], F32), ("cx", [128, R * CA], F32),
                ("lncxb", [128, R * CA], BF16), ("outp", [128, R * CA], F32),
                ("outpb", [128, R * CA], BF16),
                ("sq0", [128, CA], F32), ("sq1", [128, CA], F32),
                ("sqA0", [128, CA], F32), ("sqA1", [128, CA], F32),
                ("mx0", [128, CA], F32), ("mx1", [128, CA], F32),
                ("mg0", [128, CA], F32), ("mg1", [128, CA], F32),
                ("strow", [1, 3 * CA], F32),
            ):
                t[n] = c1.enter_context(nc.sbuf_tensor(n, s, dt))
            t["ps"] = [c1.enter_context(nc.psum_tensor("psA%d" % i, [128, 512], F32))
                       for i in range(8)]
            blk = c1.enter_context(nc.Block())
            for E in ENGS:
                p.beg(E)
                p.sem_n.update(counts)
                getattr(blk, E)(lambda eng: emit_phase_a(p, eng, t))
            counts = dict(p.sem_n)

        if not include_ffn:
            with contextlib.ExitStack() as c2:
                blk = c2.enter_context(nc.Block())
                for E in ENGS:
                    p.beg(E)
                    p.sem_n.update(counts)
                    def f(eng, E=E):
                        i = None
                        if p.on("sync"):
                            i = eng.dma_start(p.out, p.x1h)
                        p.inc("s_out", 16, i)
                        p.wait(eng, "sync", "s_out")
                    getattr(blk, E)(f)
        else:
            with contextlib.ExitStack() as c2:
                sb2 = lambda n, s_, dt=F32: c2.enter_context(nc.sbuf_tensor(n, s_, dt))
                t["idn"] = sb2("idn_s", [128, 128])
                t["ohot"] = sb2("ohot_s", [NE, NE * 128], BF16)
                t["rw_sb"] = sb2("rw_sb", [128, R * NE])
                t["x2b"] = sb2("x2b", [128, R * TQ], BF16)
                t["mixed"] = sb2("mixed", [128, R * TQ])
                t["x1t"] = sb2("x1t", [128, R * CT])
                t["hidden"] = sb2("hidden", [128, 16 * CT], BF16)
                t["gw0"] = sb2("gw0", [128, R * IE], BF16)
                t["gw1"] = sb2("gw1", [128, R * IE], BF16)
                t["uw0"] = sb2("uw0", [128, R * IE], BF16)
                t["uw1"] = sb2("uw1", [128, R * IE], BF16)
                t["dw"] = sb2("dw", [128, 16 * D], BF16)
                t["sg0"] = sb2("sg0", [128, 4 * CT], BF16)
                t["sg1"] = sb2("sg1", [128, 4 * CT], BF16)
                t["wbc_sb"] = sb2("wbc_sb", [128, 4 * CT], BF16)
                t["scores"] = sb2("scores", [NE, TQ])
                t["sfc"] = sb2("sfc", [NE, TQ])
                t["w32cm"] = sb2("w32cm", [NE, TQ], BF16)
                t["w32tm"] = sb2("w32tm", [128, NTT * 32])
                t["sfc_tm"] = sb2("sfc_tm", [128, NTT * 32])
                t["sc_tm"] = sb2("sc_tm", [128, NTT * 32])
                t["m8"] = sb2("m8", [128, 8])
                t["gm4"] = sb2("gm4", [128, 4])
                t["gs8"] = sb2("gs8", [128, 8])
                t["gidxu"] = sb2("gidxu", [128, 8], U32)
                t["gidxf"] = sb2("gidxf", [128, 8])
                t["idxu"] = sb2("idxu", [128, 8], U32)
                t["idxf"] = sb2("idxf", [128, 8])
                t["eqa"] = sb2("eqa", [128, 32])
                t["eqb"] = sb2("eqb", [128, 32])
                t["eqm"] = sb2("eqm", [128, 32])
                t["tmp32"] = sb2("tmp32", [128, 32])
                t["skc"] = sb2("skc", [128, 16])
                t["strowB"] = sb2("strowB", [1, 4 * CT])
                t["sq0"] = sb2("sqB0", [128, CT])
                t["sq1"] = sb2("sqB1", [128, CT])
                t["ps"] = [c2.enter_context(nc.psum_tensor("psB%d" % i, [128, 512], F32))
                           for i in range(8)]
                blk = c2.enter_context(nc.Block())
                for E in ENGS:
                    p.beg(E)
                    p.sem_n.update(counts)
                    def f(eng, E=E):
                        if p.on("vector"):
                            eng.memset(t["gs8"][:, 4:8], -1e30)
                            eng.drain()
                        emit_phase_b(p, eng, t)
                    getattr(blk, E)(f)
    return p


def pack_cvec(ln1_w, ln1_b, lnc_w, lnc_b, lng_w, lng_b, bg, rbias):
    cv = np.zeros((128, CV_NCOLS), np.float32)
    for base, v in ((CV_LN1W, ln1_w), (CV_LN1B, ln1_b), (CV_LNCW, lnc_w),
                    (CV_LNCB, lnc_b), (CV_LNGW, lng_w), (CV_LNGB, lng_b),
                    (CV_BG, bg)):
        cv[:, base:base + 8] = np.asarray(v, np.float32).reshape(8, 128).T
    cv[0:NE, CV_RBIAS] = np.asarray(rbias, np.float32)
    cv[:, CV_EGRP:CV_EGRP + 32] = (np.arange(32)[None, :] // 8).astype(np.float32)
    cv[:, CV_EIOTA:CV_EIOTA + 32] = np.arange(32)[None, :].astype(np.float32)
    cv[:, CV_GTINY:CV_GTINY + 4] = ((3 - np.arange(4))[None, :] * 1e-12).astype(np.float32)
    cv[:, CV_ETINY:CV_ETINY + 32] = ((31 - np.arange(32))[None, :] * 1e-12).astype(np.float32)
    return cv


def _ext_gu(ew, sw):
    ew = np.asarray(ew, np.float32).transpose(0, 2, 1)
    swT = np.asarray(sw, np.float32).T
    ps = np.stack([swT[:, 0:IE], swT[:, IE:2 * IE]], axis=0)
    return np.ascontiguousarray(np.concatenate([ew, ps], axis=0))


def _ext_d(ew, sw):
    ew = np.asarray(ew, np.float32).transpose(0, 2, 1)
    swT = np.asarray(sw, np.float32).T
    ps = np.stack([swT[0:IE, :], swT[IE:2 * IE, :]], axis=0)
    return np.ascontiguousarray(np.concatenate([ew, ps], axis=0))


def make_inputs(inp, core):
    import ml_dtypes
    x = np.asarray(inp["L0_attn_x"], np.float32).reshape(8 * TCORE, D)
    xT = np.ascontiguousarray(x[core * TCORE:(core + 1) * TCORE, :].T)
    cv = pack_cvec(inp["L0_attn_ln1_w"], inp["L0_attn_ln1_b"],
                   inp["L0_attn_lnc_w"], inp["L0_attn_lnc_b"],
                   inp["L0_attn_lng_w"], inp["L0_attn_lng_b"],
                   inp["L0_attn_bg"], inp["L0_ffn_router_bias"])
    ohot = np.zeros((NE, NE * 128), ml_dtypes.bfloat16)
    for e in range(NE):
        ohot[e, e * 128:(e + 1) * 128] = 1
    c = np.ascontiguousarray
    m = {
        "xT": xT,
        "wt": c(np.asarray(inp["L0_attn_Wt"], np.float32)),
        "wo": c(np.asarray(inp["L0_attn_Wo"], np.float32)),
        "wg": c(np.asarray(inp["L0_attn_Wg"], np.float32)),
        "rwT": c(np.asarray(inp["L0_ffn_router_weight"], np.float32).T),
        "gwT": _ext_gu(inp["L0_ffn_gate_w"], inp["L0_ffn_shared_gate_w"]),
        "uwT": _ext_gu(inp["L0_ffn_up_w"], inp["L0_ffn_shared_up_w"]),
        "dwT": _ext_d(inp["L0_ffn_down_w"], inp["L0_ffn_shared_down_w"]),
        "cvec": cv,
        "idn": np.eye(128, dtype=np.float32),
        "ohot": ohot,
    }
    return m


_PROG_CACHE = {}


def _program():
    if "p" not in _PROG_CACHE:
        _PROG_CACHE["p"] = build(include_ffn=True)
    return _PROG_CACHE["p"]


def kernel(**inputs):
    p = _program()
    shared = make_inputs(inputs, 0)
    x = np.asarray(inputs["L0_attn_x"], np.float32).reshape(8 * TCORE, D)
    in_maps = []
    for core in range(8):
        m = dict(shared)
        m["xT"] = (shared["xT"] if core == 0 else
                   np.ascontiguousarray(x[core * TCORE:(core + 1) * TCORE, :].T))
        in_maps.append(m)
    res = run_bass_kernel_spmd(p.nc, in_maps, core_ids=list(range(8)))
    out = np.empty((8 * TCORE, D), np.float32)
    for core in range(8):
        out[core * TCORE:(core + 1) * TCORE, :] = res.results[core]["out"].T
    return out.reshape(2, 2 * 8192, D)
